# version 45
# baseline (speedup 1.0000x reference)
import numpy as np
import ml_dtypes
import concourse.bass as bass
import concourse.mybir as mybir
from concourse.bass_utils import run_bass_kernel_spmd

F32 = mybir.dt.float32
BF16 = mybir.dt.bfloat16
AF = mybir.ActivationFunctionType
OP = mybir.AluOpType

BSZ, SEQ, DM = 4, 4096, 2048
FD = 2 * DM
HF = 4 * DM
TOK = 2048
T = 512
NCH = TOK // T
NBD = DM // 128
NBH = HF // 128
CQ = 0.50048828125
EPS_LN = 1e-5
EPS_RMS = 1.1920929e-7
NCORES = 8

ENGS = ("sync", "scalar", "vector", "gpsimd", "tensor")
QUEUE_SEM = {"sync": "qs", "scalar": "qc", "gpsimd": "qg"}
ENG_SEM = {"vector": "dv", "scalar": "ac", "gpsimd": "po", "tensor": "pe",
           "sync": "qs"}


class Buf:
    __slots__ = ("name", "w", "r")

    def __init__(self, name):
        self.name = name
        self.w = None
        self.r = []


NDMA = 8


class Sched:
    def __init__(self):
        self.ops = {e: [] for e in ENGS}
        self.cnt = {}
        self.rr = {}

    def _bump(self, sem, step):
        self.cnt[sem] = self.cnt.get(sem, 0) + step
        return (sem, self.cnt[sem])

    def op(self, eng, fn, reads=(), writes=(), waits=(), sem=None,
           dma=False, track=True):
        wl = list(waits)
        for b in reads:
            if b.w is not None:
                wl.append(b.w)
        for b in writes:
            if b.w is not None:
                wl.append(b.w)
            wl.extend(b.r)
        if sem is None:
            if dma:
                q = QUEUE_SEM[eng]
                i = self.rr.get(q, 0)
                self.rr[q] = (i + 1) % NDMA
                sem = f"{q}{i}"
                wl.append((sem, self.cnt.get(sem, 0)))
            else:
                sem = ENG_SEM[eng]
        step = 16 if dma else 1
        tok = self._bump(sem, step) if track else None
        self.ops[eng].append((tuple(wl), fn, sem if track else None, step))
        if track:
            for b in reads:
                b.r.append(tok)
            for b in writes:
                b.w = tok
                b.r = []
        return tok

    def dmaop(self, eng, fn, reads=(), writes=(), waits=()):
        return self.op(eng, fn, reads, writes, waits, dma=True)

    def tok(self, sem):
        return (sem, self.cnt.get(sem, 0))

    def qtoks(self, q):
        return [(f"{q}{i}", self.cnt.get(f"{q}{i}", 0)) for i in range(NDMA)]

    def replay(self, ename, eng, sems):
        seen = {}
        for waits, fn, sem, step in self.ops[ename]:
            for (s, v) in waits:
                if v > 0 and seen.get(s, 0) < v:
                    eng.wait_ge(sems[s], v)
                    seen[s] = v
            instr = fn(eng)
            if sem is not None:
                instr.then_inc(sems[sem], step)


def build(debug=False):
    nc = bass.Bass("TRN2", target_bir_lowering=False, debug=False)

    xs_d = nc.dram_tensor("xs", [DM, TOK], F32, kind="ExternalInput").ap()
    wslab = nc.dram_tensor("wslab", [16, FD * 16], F32,
                           kind="ExternalInput").ap()
    gslab = nc.dram_tensor("gslab", [256, 2 * HF], F32,
                           kind="ExternalInput").ap()
    pslab = nc.dram_tensor("pslab", [1024, DM], F32, kind="ExternalInput").ap()
    apar = nc.dram_tensor("apar", [128, 1], F32, kind="ExternalInput").ap()
    sel_d = nc.dram_tensor("sel", [8, 1], F32, kind="ExternalInput").ap()
    expq_d = nc.dram_tensor("expq_in", [2, 128], BF16,
                            kind="ExternalInput").ap()
    out_d = nc.dram_tensor("outT", [DM, TOK], F32, kind="ExternalOutput").ap()

    wc_d = nc.dram_tensor("wc", [16, FD * 16], BF16)
    wg_d = nc.dram_tensor("wg", [128, FD * 16], BF16, addr_space="Shared")
    gc_d = nc.dram_tensor("gc", [16, 2 * HF * 16], BF16)
    gq_d = nc.dram_tensor("gq", [128, 2 * HF * 16], BF16,
                          addr_space="Shared")
    pc_d = nc.dram_tensor("pc", [1024, DM], BF16)
    pq_d = nc.dram_tensor("pq", [HF, DM], BF16, addr_space="Shared")
    srg_d = nc.dram_tensor("srg", [4, 2 * HF], BF16)
    srp_d = nc.dram_tensor("srp", [16, DM], BF16)
    carc_d = nc.dram_tensor("carc", [128, 16], F32)
    carg_d = nc.dram_tensor("carg", [8 * 128, 16], F32, addr_space="Shared")
    car1_d = nc.dram_tensor("car1", [1, DM], F32)
    kind_scr = "ExternalOutput" if debug else "Internal"
    h_d = nc.dram_tensor("hscr", [DM, TOK], F32, kind=kind_scr)
    e_d = nc.dram_tensor("escr", [DM, TOK], F32, kind=kind_scr)
    xm_d = nc.dram_tensor("xmscr", [DM, TOK], F32, kind=kind_scr)
    mrows_d = nc.dram_tensor("mrows", [8, T], F32)

    from contextlib import ExitStack
    with ExitStack() as stk:
        def sb(name, shape, dtype):
            return stk.enter_context(nc.sbuf_tensor(name, shape, dtype))

        apar_t = sb("apar_t", [128, 1], F32)
        sneg = sb("sneg", [128, 1], F32)
        hsneg = sb("hsneg", [128, 1], F32)
        selt = sb("selt", [8, 1], F32)
        carry_t = sb("carry_t", [128, 16], F32)
        zeros_t = sb("zeros_t", [128, T], F32)
        ones1 = sb("ones1", [1, 128], F32)
        onesb = sb("onesb", [128, 1], BF16)
        onesf = sb("onesf", [128, 1], F32)
        blkq = sb("blkq", [128, 2], BF16)
        expq = sb("expq", [2, 128], BF16)
        invr_t = sb("invr_t", [128, T], F32)
        rowA = sb("rowA", [1, T], F32)
        rowB = sb("rowB", [1, T], F32)
        rowC = sb("rowC", [1, T], F32)
        srow_t = sb("srow_t", [2, 4096], BF16)
        cst = sb("cst", [128, 2], F32)
        carh_t = [sb(f"carh{i}", [128, 16], F32) for i in range(2)]
        carC_t = [sb(f"carC{i}", [128, 16], F32) for i in range(2)]
        cart_t = sb("cart", [128, 16], F32)

        xa = sb("xa", [128, NBD * T], F32)
        ln1 = sb("ln1", [128, NBD * T], BF16)
        wt = [sb(f"wt{i}", [128, 2048], BF16) for i in range(6)]
        bxs = sb("bxs", [128, NBD * T], BF16)
        abf = sb("abf", [128, NBD * T], BF16)
        ybf = sb("ybf", [128, NBD * T], BF16)
        NROT = 13
        rot = [sb(f"rot{i}", [128, T], F32) for i in range(NROT)]
        brot = [sb(f"brot{i}", [128, T], BF16) for i in range(4)]

        ffn = sb("ffn", [128, NBD * T], F32)
        wqp = [sb(f"wqp{i}", [128, 1024], BF16) for i in range(3)]
        sqg = [zeros_t[:, 0:256].bitcast(BF16),
               zeros_t[:, 256:512].bitcast(BF16)]

        hb = [ln1, abf, ybf, bxs]
        agc_t = xa[0:8, 0:DM]
        crow = ln1[0:1, 0:2 * DM].bitcast(F32)
        ln2_p = [ffn[:, 0:4096].bitcast(BF16),
                 ffn[:, 4096:8192].bitcast(BF16)]
        q_wbf_ap = [ffn[:, 4096 + i * 1024:4096 + (i + 1) * 1024].bitcast(BF16)
                    for i in range(2)]
        q_A_ap2 = [ffn[:, 6144:7168].bitcast(BF16),
                   ffn[:, 1024:2048].bitcast(BF16)]
        q_B_ap2 = [ffn[:, 7168:8192].bitcast(BF16),
                   ffn[:, 2048:3072].bitcast(BF16)]
        q_C_ap2 = [ffn[:, 0:1024].bitcast(BF16),
                   ffn[:, 3072:4096].bitcast(BF16)]

        psum = [stk.enter_context(nc.psum_tensor(f"pb{i}", [128, 512], F32))
                for i in range(8)]

        sem_names = (["pe", "dv", "ac", "po", "cc"] +
                     [f"{q}{i}" for q in ("qs", "qc", "qg")
                      for i in range(NDMA)])
        sems = {n: stk.enter_context(nc.semaphore(n)) for n in sem_names}
        S = Sched()

        b_rot = [Buf(f"rot{i}") for i in range(NROT)]
        b_brot = [Buf(f"brot{i}") for i in range(4)]
        b_xa = Buf("xa")
        b_ln1 = Buf("ln1")
        b_wt = [Buf(f"wt{i}") for i in range(6)]
        b_bxs = Buf("bxs")
        b_abf = Buf("abf")
        b_ybf = Buf("ybf")
        b_ffn = Buf("ffn")
        b_ln2 = [Buf("ln2a"), Buf("ln2b")]
        b_sqg = [Buf("sqg0"), Buf("sqg1")]
        b_wqp = [Buf(f"wqp{i}") for i in range(3)]
        b_ps = [Buf(f"ps{i}") for i in range(8)]
        b_hb = [b_ln1, b_abf, b_ybf, b_bxs]
        b_rows = Buf("rows")
        b_srow = Buf("srow")
        b_carh = [Buf(f"carh{i}") for i in range(2)]
        b_carC = [Buf(f"carC{i}") for i in range(2)]
        b_cart = Buf("cart")
        b_carry = Buf("carry")
        b_invr = Buf("invr")
        b_misc = Buf("misc")
        b_qw = [Buf("qw0"), Buf("qw1")]
        b_qA2 = [Buf("qA0"), Buf("qA1")]
        b_qB2 = [Buf("qB0"), Buf("qB1")]
        b_qC2 = [Buf("qC0"), Buf("qC1")]
        b_srow2 = [b_srow, Buf("srow1")]

        t_sel = S.dmaop("sync", lambda e: e.dma_start(out=selt[:, :],
                                                      in_=sel_d))
        t_ap = S.dmaop("sync", lambda e: e.dma_start(out=apar_t[:, :],
                                                     in_=apar))
        S.op("vector", lambda e: e.memset(zeros_t[:, :], 0.0),
             writes=[b_misc])
        S.op("vector", lambda e: e.memset(ones1[:, :], 1.0), writes=[b_misc])
        S.op("vector", lambda e: e.memset(onesb[:, :], 1.0), writes=[b_misc])
        S.op("vector", lambda e: e.memset(onesf[:, :], 1.0), writes=[b_misc])
        S.op("vector", lambda e: e.memset(blkq[:, :], 0.0), writes=[b_misc])
        S.op("vector", lambda e: e.memset(blkq[0:64, 0:1], 1.0),
             writes=[b_misc])
        S.dmaop("sync", lambda e: e.dma_start(out=expq[:, :], in_=expq_d))
        S.op("vector", lambda e: e.memset(blkq[64:128, 1:2], 1.0),
             writes=[b_misc])
        S.op("vector", lambda e: e.memset(cst[:, 0:1], EPS_LN),
             writes=[b_misc])
        t_const = S.op("vector", lambda e: e.memset(cst[:, 1:2], EPS_RMS),
                       writes=[b_misc])

        S.op("vector", lambda e: e.scalar_tensor_tensor(
            rot[0][:, 0:1], apar_t[:, 0:1], -1.0, apar_t[:, 0:1],
            OP.mult, OP.max), waits=[t_ap], writes=[b_rot[0]])
        S.op("scalar", lambda e: e.activation(
            rot[1][:, 0:1], rot[0][:, 0:1], AF.Exp, scale=-1.0),
            reads=[b_rot[0]], writes=[b_rot[1]])
        S.op("scalar", lambda e: e.activation(
            rot[2][:, 0:1], rot[1][:, 0:1], AF.Ln, bias=1.0),
            reads=[b_rot[1]], writes=[b_rot[2]])
        S.op("vector", lambda e: e.tensor_scalar_max(
            rot[3][:, 0:1], apar_t[:, 0:1], 0.0), waits=[t_ap],
            writes=[b_rot[3]])
        S.op("vector", lambda e: e.tensor_add(
            rot[3][:, 0:1], rot[3][:, 0:1], rot[2][:, 0:1]),
            reads=[b_rot[2]], writes=[b_rot[3]])
        S.op("vector", lambda e: e.tensor_scalar_mul(
            sneg[:, 0:1], rot[3][:, 0:1], -8.0), reads=[b_rot[3]],
            writes=[b_misc])
        t_sneg = S.op("vector", lambda e: e.tensor_scalar_mul(
            hsneg[:, 0:1], rot[3][:, 0:1], -4.0), reads=[b_rot[3]],
            writes=[b_misc])

        t_wc = S.dmaop("gpsimd", lambda e: e.dma_start(out=wc_d.ap(),
                                                       in_=wslab))
        t_ag1 = S.op("gpsimd", lambda e: e.collective_compute(
            "AllGather", OP.bypass, replica_groups=[list(range(NCORES))],
            ins=[wc_d.ap()], outs=[wg_d.ap()]), waits=[t_wc], sem="cc")

        QW = 2048
        qpieces = []
        for pbk in range(2):
            for cs in range(2 * HF // QW):
                qpieces.append(("g", pbk, cs, gslab, gc_d, srg_d))
        for pbk in range(8):
            qpieces.append(("p", pbk, 0, pslab, pc_d, srp_d))
        gstores, pstores = [], []

        def qctx(qis):
            ctx = []
            for qi in qis:
                kind, pbk, cs, src_d, dst_d, sr_d = qpieces[qi]
                par = qi % 2
                ctx.append(dict(
                    kind=kind, pbk=pbk, cs=cs,
                    r0=pbk * 128, c0=cs * QW, src_d=src_d,
                    dst_d=dst_d, wb=q_wbf_ap[par], wbB=b_qw[par],
                    qA=q_A_ap2[par], qAb=b_qA2[par],
                    qB=q_B_ap2[par], qBb=b_qB2[par],
                    qC=q_C_ap2[par], qCb=b_qC2[par],
                    sof=par * 2048, srb=b_srow2[par]))
            return ctx

        def emit_qstep(ctx, step):
            if step == 0:
                for c in ctx:
                    S.dmaop("gpsimd", lambda e, c=c: e.dma_start(
                        out=c["wb"], in_=c["src_d"][c["r0"]:c["r0"] + 128,
                                                   c["c0"]:c["c0"] + QW]),
                        writes=[c["wbB"]])
            elif step == 1:
                for ci, c in enumerate(ctx):
                    if ci == 0:
                        S.op("vector", lambda e, c=c: e.scalar_tensor_tensor(
                            c["qA"], c["wb"], -1.0, c["wb"],
                            OP.mult, OP.max),
                            reads=[c["wbB"]], writes=[c["qAb"]])
                    else:
                        S.op("scalar", lambda e, c=c: e.activation(
                            c["qA"], c["wb"], AF.Abs),
                            reads=[c["wbB"]], writes=[c["qAb"]])
            elif step == 2:
                for ci, c in enumerate(ctx):
                    for k in range(QW // 512):
                        pb = 5 + (2 * ci + k) % 2
                        S.op("tensor", lambda e, c=c, k=k, pb=pb: e.matmul(
                            psum[pb][0:2, :], blkq[:, :],
                            c["qA"][:, k * 512:(k + 1) * 512],
                            start=True, stop=True), waits=[t_const],
                            reads=[c["qAb"]], writes=[b_ps[pb]])
                        if k % 2 == 0:
                            S.op("vector", lambda e, c=c, k=k, pb=pb:
                                 e.tensor_scalar(
                                srow_t[0:2, c["sof"] + k * 512:
                                       c["sof"] + (k + 1) * 512],
                                psum[pb][0:2, :], 1.0 / 64.0, 1e-8,
                                OP.mult, OP.max),
                                reads=[b_ps[pb]], writes=[c["srb"]])
                        else:
                            S.op("scalar", lambda e, c=c, k=k, pb=pb:
                                 e.activation(
                                srow_t[0:2, c["sof"] + k * 512:
                                       c["sof"] + (k + 1) * 512],
                                psum[pb][0:2, :], AF.Copy,
                                scale=1.0 / 64.0),
                                reads=[b_ps[pb]], writes=[c["srb"]])
            elif step == 3:
                for ci, c in enumerate(ctx):
                    for k in range(QW // 512):
                        pb = 5 + (2 * ci + k) % 2
                        S.op("tensor", lambda e, c=c, k=k, pb=pb: e.matmul(
                            psum[pb][:, :], expq[:, :],
                            srow_t[0:2, c["sof"] + k * 512:
                                   c["sof"] + (k + 1) * 512],
                            start=True, stop=True), waits=[t_const],
                            reads=[c["srb"]], writes=[b_ps[pb]])
                        if k % 2 == 0:
                            S.op("vector", lambda e, c=c, k=k, pb=pb:
                                 e.tensor_copy(
                                c["qC"][:, k * 512:(k + 1) * 512],
                                psum[pb][:, :]),
                                reads=[b_ps[pb]], writes=[c["qCb"]])
                        else:
                            S.op("scalar", lambda e, c=c, k=k, pb=pb:
                                 e.activation(
                                c["qC"][:, k * 512:(k + 1) * 512],
                                psum[pb][:, :], AF.Copy),
                                reads=[b_ps[pb]], writes=[c["qCb"]])
            elif step == 4:
                for c in ctx:
                    S.op("vector", lambda e, c=c: e.scalar_tensor_tensor(
                        c["qB"], c["qC"], CQ, c["qA"], OP.mult, OP.is_lt),
                        reads=[c["qCb"], c["qAb"]], writes=[c["qBb"]])
            elif step == 5:
                for c in ctx:
                    S.op("scalar", lambda e, c=c: e.activation(
                        c["qA"], c["wb"], AF.Sign),
                        reads=[c["wbB"]], writes=[c["qAb"]])
                for c in ctx:
                    S.op("vector", lambda e, c=c: e.tensor_mul(
                        c["qB"], c["qB"], c["qC"]),
                        reads=[c["qCb"]], writes=[c["qBb"]])
            elif step == 6:
                for c in ctx:
                    S.op("vector", lambda e, c=c: e.tensor_mul(
                        c["qB"], c["qB"], c["qA"]),
                        reads=[c["qAb"]], writes=[c["qBb"]])
            elif step == 7:
                for c in ctx:
                    if c["kind"] == "g":
                        for bl in range(16):
                            beta = c["cs"] * 16 + bl
                            t_st = S.dmaop(
                                "scalar", lambda e, c=c, bl=bl, beta=beta:
                                e.dma_start(
                                    out=c["dst_d"].ap()
                                    [c["pbk"] * 8:(c["pbk"] + 1) * 8,
                                     beta * 2048:(beta + 1) * 2048],
                                    in_=c["qB"][:, bl * 128:(bl + 1) * 128]),
                                reads=[c["qBb"]])
                            gstores.append(t_st)
                    else:
                        t_st = S.dmaop("scalar", lambda e, c=c: e.dma_start(
                            out=c["dst_d"].ap()[c["r0"]:c["r0"] + 128,
                                                c["c0"]:c["c0"] + QW],
                            in_=c["qB"]), reads=[c["qBb"]])
                        pstores.append(t_st)

        def emit_ag2():
            return S.op("gpsimd", lambda e: e.collective_compute(
                "AllGather", OP.bypass, replica_groups=[list(range(NCORES))],
                ins=[gc_d.ap()], outs=[gq_d.ap()]), waits=gstores, sem="cc")

        def emit_ag3():
            return S.op("gpsimd", lambda e: e.collective_compute(
                "AllGather", OP.bypass, replica_groups=[list(range(NCORES))],
                ins=[pc_d.ap()], outs=[pq_d.ap()]), waits=pstores, sem="cc")

        def stats_cpsq(src, src_buf, b, eng="gpsimd"):
            sq, sqB = brot[b % 4], b_brot[b % 4]
            S.op(eng, lambda e, src=src, sq=sq: e.tensor_mul(
                sq[:, :], src, src), reads=[src_buf], writes=[sqB])

        def stats_mm(src, src_buf, b, first, last, bs=3, bq=4):
            sq, sqB = brot[b % 4], b_brot[b % 4]
            bw3 = ([b_ps[bs].w] if first and b_ps[bs].w else []) + \
                  (list(b_ps[bs].r) if first else [])
            bw4 = ([b_ps[bq].w] if first and b_ps[bq].w else []) + \
                  (list(b_ps[bq].r) if first else [])
            S.op("tensor", lambda e, src=src, first=first, last=last, bs=bs:
                 e.matmul(psum[bs][0:1, :], onesf[:, 0:1], src,
                          start=first, stop=last),
                 waits=[t_const] + bw3, reads=[src_buf],
                 writes=[b_ps[bs]] if last else [])
            S.op("tensor", lambda e, sq=sq, first=first, last=last, bq=bq:
                 e.matmul(psum[bq][0:1, :], onesb[:, 0:1], sq[:, :],
                          start=first, stop=last),
                 waits=bw4, reads=[sqB],
                 writes=[b_ps[bq]] if last else [])
            if not last:
                b_ps[bs].w = S.tok("pe")
                b_ps[bq].w = S.tok("pe")

        def stats_block(src, src_buf, b, first, last, eng="gpsimd"):
            stats_cpsq(src, src_buf, b, eng=eng)
            stats_mm(src, src_buf, b, first, last)

        def stats(src_ap_fn, src_buf, n_blocks, eng="gpsimd"):
            for b in range(n_blocks):
                stats_block(src_ap_fn(b), src_buf, b,
                            b == 0, b == n_blocks - 1, eng=eng)

        def ln_rows(n, eps_i=0, bs=3, bq=4):
            S.op("vector", lambda e, bs=bs: e.tensor_scalar_mul(
                rowA[0:1, :], psum[bs][0:1, :], 1.0 / n),
                reads=[b_ps[bs]], writes=[b_rows])
            S.op("vector", lambda e, bq=bq: e.tensor_scalar_mul(
                rowB[0:1, :], psum[bq][0:1, :], 1.0 / n),
                reads=[b_ps[bq]], writes=[b_rows])
            S.op("vector", lambda e: e.tensor_mul(
                rowC[0:1, :], rowA[0:1, :], rowA[0:1, :]), writes=[b_rows])
            S.op("vector", lambda e: e.tensor_sub(
                rowB[0:1, :], rowB[0:1, :], rowC[0:1, :]), writes=[b_rows])
            S.op("scalar", lambda e, eps_i=eps_i: e.activation(
                rowC[0:1, :], rowB[0:1, :], AF.Ln, bias=cst[0:1, eps_i:eps_i + 1]),
                waits=[t_const], reads=[b_rows], writes=[b_rows])
            return S.op("scalar", lambda e: e.activation(
                rowB[0:1, :], rowC[0:1, :], AF.Exp, scale=-0.5),
                reads=[b_rows], writes=[b_rows])

        def bcast(dst_i, row_t):
            return S.op("tensor", lambda e, dst_i=dst_i, row_t=row_t:
                        e.matmul(
                psum[dst_i][:, :], ones1[0:1, :], row_t[0:1, :],
                start=True, stop=True), reads=[b_rows],
                writes=[b_ps[dst_i]], waits=[t_const])

        def mm_group(bank_i, lhsT_fn, rhs_fn, rbufs, nk, first_grp=True,
                     last_grp=True, extra_first=()):
            for k in range(nk):
                first, last = (k == 0 and first_grp), \
                    (k == nk - 1 and last_grp)
                waits = list(extra_first) if k == 0 else []
                if first:
                    if b_ps[bank_i].w:
                        waits.append(b_ps[bank_i].w)
                    waits.extend(b_ps[bank_i].r)
                    for rb in rbufs:
                        if rb.w:
                            waits.append(rb.w)
                S.op("tensor", lambda e, k=k: e.matmul(
                    psum[bank_i][:, :], lhsT_fn(k), rhs_fn(k),
                    start=(k == 0 and first_grp),
                    stop=(k == nk - 1 and last_grp)),
                    waits=waits, reads=list(rbufs) if last else [],
                    writes=[b_ps[bank_i]] if last else [],
                    track=last)
                if not last:
                    b_ps[bank_i].w = S.tok("pe")

        hstore, estore = [], []

        def xa_load(j, src, src_waits, qlay=False):
            pat = "(q e) s -> q e s" if qlay else "(n p) s -> p n s"
            kw = {"e": NBD} if qlay else {"p": 128}
            return S.dmaop("scalar", lambda e, j=j, src=src: e.dma_start(
                out=xa[:, :].rearrange("p (n t) -> p n t", n=NBD),
                in_=src.rearrange(pat, **kw)[:, :, j * T:(j + 1) * T]),
                waits=src_waits, writes=[b_xa])

        mrow_st = {}

        def ln1_apply(j):
            S.dmaop("sync", lambda e, j=j: e.dma_start(
                out=rowA[0:1, :], in_=mrows_d.ap()[2 * j:2 * j + 1, :]),
                waits=mrow_st[j], writes=[b_rows])
            S.dmaop("sync", lambda e, j=j: e.dma_start(
                out=rowB[0:1, :], in_=mrows_d.ap()[2 * j + 1:2 * j + 2, :]),
                waits=mrow_st[j], writes=[b_rows])
            bcast(7, rowA)
            bcast(4, rowB)
            for b in range(NBD):
                rr, rb = rot[b % 2], b_rot[b % 2]
                S.op("vector", lambda e, b=b, rr=rr: e.tensor_sub(
                    rr[:, :], xa[:, b * T:(b + 1) * T], psum[7][:, :]),
                    reads=[b_xa, b_ps[7]], writes=[rb])
                S.op("vector", lambda e, b=b, rr=rr: e.tensor_mul(
                    ln1[:, b * T:(b + 1) * T], rr[:, :], psum[4][:, :]),
                    reads=[rb, b_ps[4]], writes=[b_ln1])

        for jj in (1, 2, 3, 0):
            xa_load(jj, xs_d, [])
            stats(lambda b: xa[:, b * T:(b + 1) * T], b_xa, NBD,
                  eng="vector")
            ln_rows(DM)
            mrow_st[jj] = [
                S.dmaop("gpsimd", lambda e, jj=jj: e.dma_start(
                    out=mrows_d.ap()[2 * jj:2 * jj + 1, :],
                    in_=rowA[0:1, :]), reads=[b_rows]),
                S.dmaop("gpsimd", lambda e, jj=jj: e.dma_start(
                    out=mrows_d.ap()[2 * jj + 1:2 * jj + 2, :],
                    in_=rowB[0:1, :]), reads=[b_rows])]
        ln1_apply(0)
        xa_load(1, xs_d, [])
        for j in range(NCH):
            qpp0 = qctx([6 * j, 6 * j + 1])
            qpp1 = qctx([6 * j + 2, 6 * j + 3])
            qpp2 = qctx([6 * j + 4, 6 * j + 5])
            for b in range(NBD):
                if b == 0:
                    gx_i, ga_i = 5, 6
                elif b == 1:
                    gx_i, ga_i = 4, 7
                else:
                    gx_i = (2 * b) % 4
                    ga_i = (2 * b + 1) % 4
                wgx_i = (2 * b) % 6
                wga_i = (2 * b + 1) % 6
                S.dmaop("sync", lambda e, b=b, wgx_i=wgx_i: e.dma_start(
                    out=wt[wgx_i][:, :],
                    in_=wg_d.ap()[:, 2048 * b:2048 * (b + 1)]),
                    waits=[t_ag1], writes=[b_wt[wgx_i]])
                S.dmaop("sync", lambda e, b=b, wga_i=wga_i: e.dma_start(
                    out=wt[wga_i][:, :],
                    in_=wg_d.ap()[:, 2048 * (16 + b):2048 * (17 + b)]),
                    waits=[t_ag1], writes=[b_wt[wga_i]])
                for bank_i, wi in ((gx_i, wgx_i), (ga_i, wga_i)):
                    mm_group(bank_i,
                             (lambda wi: lambda k:
                              wt[wi][:, k * 128:(k + 1) * 128])(wi),
                             lambda k: ln1[:, k * T:(k + 1) * T],
                             [b_wt[wi], b_ln1], NBD)
                tg, tgB = rot[2 + b % 2], b_rot[2 + b % 2]
                tg2, tg2B = rot[9], b_rot[9]
                gs, gsB = rot[4], b_rot[4]
                af_, afB = rot[6 + b % 2], b_rot[6 + b % 2]
                la_, laB = rot[8], b_rot[8]
                a2_, a2B = rot[2 + b % 2], b_rot[2 + b % 2]
                C_, CB = rot[11 + b % 2], b_rot[11 + b % 2]
                ec_, ecB = rot[10], b_rot[10]
                S.op("scalar", lambda e, gx_i=gx_i, tg=tg: e.activation(
                    tg[:, :], psum[gx_i][:, :], AF.Tanh, scale=0.5),
                    reads=[b_ps[gx_i]], writes=[tgB])
                S.op("gpsimd", lambda e, tg=tg, gs=gs: e.tensor_scalar(
                    gs[:, :], tg[:, :], 0.5, 0.5, OP.mult, OP.add),
                    reads=[tgB], writes=[gsB])
                S.op("scalar", lambda e, ga_i=ga_i, tg2=tg2: e.activation(
                    tg2[:, :], psum[ga_i][:, :], AF.Tanh, scale=0.5),
                    reads=[b_ps[ga_i]], writes=[tg2B])
                S.op("scalar", lambda e, tg2=tg2, af_=af_: e.activation(
                    af_[:, :], tg2[:, :], AF.Exp, scale=hsneg[:, 0:1],
                    bias=hsneg[:, 0:1]),
                    waits=[t_sneg], reads=[tg2B], writes=[afB])
                S.op("vector", lambda e, tg2=tg2, la_=la_: e.tensor_scalar(
                    la_[:, :], tg2[:, :], hsneg[:, 0:1], hsneg[:, 0:1],
                    OP.mult, OP.add),
                    waits=[t_sneg], reads=[tg2B], writes=[laB])
                cin = 0.0 if j == 0 else carC_t[(j + 1) % 2][:, b:b + 1]
                crd = [] if j == 0 else [b_carC[(j + 1) % 2]]
                S.op("vector", lambda e, la_=la_, cin=cin, C_=C_:
                     e.tensor_tensor_scan(C_[:, :], la_[:, :], zeros_t[:, :],
                                          cin, OP.add, OP.add),
                     waits=[t_const], reads=[laB] + crd, writes=[CB])
                if j + 1 < NCH:
                    S.op("vector", lambda e, b=b, j=j, C_=C_: e.tensor_copy(
                        carC_t[j % 2][:, b:b + 1], C_[:, 511:512]),
                        reads=[CB], writes=[b_carC[j % 2]])
                S.op("scalar", lambda e, C_=C_, ec_=ec_: e.activation(
                    ec_[:, :], C_[:, :], AF.Exp), reads=[CB], writes=[ecB])
                S.op("scalar", lambda e, af_=af_, a2_=a2_: e.activation(
                    a2_[:, :], af_[:, :], AF.Square),
                    reads=[afB], writes=[a2B])
                S.op("vector", lambda e, b=b, gs=gs: e.tensor_mul(
                    bxs[:, b * T:(b + 1) * T], gs[:, :],
                    ln1[:, b * T:(b + 1) * T]),
                    reads=[gsB, b_ln1], writes=[b_bxs])
                S.op("gpsimd", lambda e, b=b, af_=af_: e.tensor_copy(
                    abf[:, b * T:(b + 1) * T], af_[:, :]),
                    reads=[afB], writes=[b_abf])
                S.op("vector", lambda e, b=b, a2_=a2_: e.tensor_scalar(
                    ybf[:, b * T:(b + 1) * T], a2_[:, :], -1.0, 1.0,
                    OP.mult, OP.add), reads=[a2B], writes=[b_ybf])
                estore.append(S.dmaop(
                    "gpsimd", lambda e, b=b, j=j, ec_=ec_: e.dma_start(
                        out=e_d.ap()[b * 128:(b + 1) * 128,
                                     j * T:(j + 1) * T],
                        in_=ec_[:, :]), reads=[ecB]))
                if b < 8:
                    emit_qstep(qpp0, b)
                else:
                    emit_qstep(qpp1, b - 8)
                    if b >= 14:
                        emit_qstep(qpp2, b - 14)

            if j == 2:
                t_ag2 = emit_ag2()

            if j + 1 < NCH:
                ln1_apply(j + 1)
                if j + 2 < NCH:
                    xa_load(j + 2, xs_d, [])

            for b in range(NBD):
                lnv, lnvB = rot[2 + b % 2], b_rot[2 + b % 2]
                mlt, mltB = rot[5 + 5 * (b % 2)], b_rot[5 + 5 * (b % 2)]
                bfv, bfvB = (rot[8], b_rot[8]) if b % 2 == 0 else \
                    (invr_t, b_invr)
                a32, a32B = rot[9 + (b % 2) * 4 - (b % 2) * 9 + (b % 2) * 4]\
                    if False else (rot[4 if b % 2 else 9],
                                   b_rot[4 if b % 2 else 9])
                a32, a32B = (rot[4], b_rot[4]) if b % 2 else \
                    (rot[9], b_rot[9])
                hh, hhB = rot[6 + b % 2], b_rot[6 + b % 2]
                S.op("scalar", lambda e, b=b, lnv=lnv: e.activation(
                    lnv[:, :], ybf[:, b * T:(b + 1) * T], AF.Ln),
                    reads=[b_ybf], writes=[lnvB])
                S.op("scalar", lambda e, lnv=lnv, mlt=mlt: e.activation(
                    mlt[:, :], lnv[:, :], AF.Exp, scale=0.5),
                    reads=[lnvB], writes=[mltB])
                S.op("gpsimd", lambda e, b=b, mlt=mlt, bfv=bfv:
                     e.tensor_mul(bfv[:, :], mlt[:, :],
                                  bxs[:, b * T:(b + 1) * T]),
                     reads=[mltB, b_bxs], writes=[bfvB])
                S.op("vector", lambda e, b=b, a32=a32: e.tensor_copy(
                    a32[:, :], abf[:, b * T:(b + 1) * T]),
                    reads=[b_abf], writes=[a32B])
                hin = 0.0 if j == 0 else carh_t[(j + 1) % 2][:, b:b + 1]
                hrd = [] if j == 0 else [b_carh[(j + 1) % 2]]
                S.op("vector", lambda e, a32=a32, bfv=bfv, hin=hin, hh=hh:
                     e.tensor_tensor_scan(hh[:, :], a32[:, :], bfv[:, :],
                                          hin, OP.mult, OP.add),
                     reads=[a32B, bfvB] + hrd, writes=[hhB])
                if j + 1 < NCH:
                    S.op("vector", lambda e, b=b, j=j, hh=hh: e.tensor_copy(
                        carh_t[j % 2][:, b:b + 1], hh[:, 511:512]),
                        reads=[hhB], writes=[b_carh[j % 2]])
                else:
                    S.op("vector", lambda e, b=b, hh=hh: e.tensor_copy(
                        cart_t[:, b:b + 1], hh[:, 511:512]),
                        reads=[hhB], writes=[b_cart])
                hstore.append(S.dmaop(
                    "gpsimd", lambda e, b=b, j=j, hh=hh: e.dma_start(
                        out=h_d.ap()[b * 128:(b + 1) * 128,
                                     j * T:(j + 1) * T],
                        in_=hh[:, :]), reads=[hhB]))
                if b < 6:
                    emit_qstep(qpp2, b + 2)
            if j == NCH - 1:
                t_ag3 = emit_ag3()

        t_cart = S.dmaop("gpsimd", lambda e: e.dma_start(
            out=carc_d.ap(), in_=cart_t[:, :]), reads=[b_cart])
        t_ag4 = S.op("gpsimd", lambda e: e.collective_compute(
            "AllGather", OP.bypass, replica_groups=[list(range(NCORES))],
            ins=[carc_d.ap()], outs=[carg_d.ap()]), waits=[t_cart], sem="cc")
        S.dmaop("sync", lambda e: e.dma_start(
            out=agc_t,
            in_=carg_d.ap().rearrange("(r p) b -> r (p b)", r=8)),
            waits=[t_ag4], writes=[b_xa])
        for k in range(4):
            S.op("tensor", lambda e, k=k: e.matmul(
                psum[5][0:1, :], selt[:, 0:1],
                agc_t[:, k * 512:(k + 1) * 512], start=True, stop=True),
                waits=[t_sel] + ([b_ps[5].w] if b_ps[5].w else []) +
                list(b_ps[5].r), reads=[b_xa], writes=[b_ps[5]])
            S.op("vector", lambda e, k=k: e.tensor_copy(
                crow[0:1, k * 512:(k + 1) * 512], psum[5][0:1, :]),
                reads=[b_ps[5]], writes=[b_ln1])
        t_c1 = S.dmaop("gpsimd", lambda e: e.dma_start(
            out=car1_d.ap(), in_=crow[0:1, :]), reads=[b_ln1])
        S.dmaop("sync", lambda e: e.dma_start(
            out=carry_t[:, :],
            in_=car1_d.ap().rearrange("o (ql e qh) -> o qh ql e",
                                      ql=8, e=16, qh=16),
            allow_slow_non_contiguous=True),
            waits=[t_c1], writes=[b_carry])

        gate_all = [S.tok(s) for s in ("pe", "dv", "ac", "po")]
        ffn_gate = gate_all + S.qtoks("qs") + S.qtoks("qc") + S.qtoks("qg")

        FBS, FBQ, FBM = 3, 5, 6
        xmst = {}

        def f_stats_tasks(src_fn, src_buf):
            ts = []
            for k in range(10):
                def t(k=k):
                    if k >= 2:
                        for bb in (2 * (k - 2), 2 * (k - 2) + 1):
                            stats_mm(src_fn(bb), src_buf, bb,
                                     bb == 0, bb == NBD - 1,
                                     bs=FBS, bq=FBQ)
                    if k < 8:
                        for bb in (2 * k, 2 * k + 1):
                            stats_cpsq(src_fn(bb), src_buf, bb)
                ts.append(t)
            return ts

        def f_rows_tasks():
            def t0():
                ln_rows(DM, bs=FBS, bq=FBQ)

            def t1():
                bcast(FBM, rowA)
                bcast(FBS, rowB)
            return [t0, t1]

        def s1_tasks(jc, gate, batch_e=False):
            ts = []

            def t0(jc=jc, gate=gate):
                xa_load(jc, h_d.ap(),
                        list(gate) + hstore[16 * jc:16 * (jc + 1)],
                        qlay=True)
                if batch_e:
                    S.dmaop("scalar", lambda e, jc=jc: e.dma_start(
                        out=ffn[:, :].rearrange("p (c t) -> p c t", c=NBD),
                        in_=e_d.ap().rearrange("(q c) t -> q c t", c=NBD)
                            [:, :, jc * T:(jc + 1) * T]),
                        waits=list(gate) + estore[16 * jc:16 * (jc + 1)],
                        writes=[b_ffn])
            ts.append(t0)
            for k in range(8):
                def t(k=k, jc=jc, gate=gate):
                    for bb in (2 * k, 2 * k + 1):
                        if batch_e:
                            S.op("vector", lambda e, bb=bb:
                                 e.scalar_tensor_tensor(
                                xa[:, bb * T:(bb + 1) * T],
                                ffn[:, bb * T:(bb + 1) * T],
                                carry_t[:, bb:bb + 1],
                                xa[:, bb * T:(bb + 1) * T],
                                OP.mult, OP.add),
                                reads=[b_ffn, b_carry], writes=[b_xa])
                            continue
                        eb, ebB = rot[bb % 2], b_rot[bb % 2]
                        S.dmaop("scalar", lambda e, bb=bb, eb=eb, jc=jc:
                                e.dma_start(
                            out=eb[:, :],
                            in_=e_d.ap().rearrange(
                                "(q e) t -> q e t", e=NBD)
                                [:, bb:bb + 1, jc * T:(jc + 1) * T]),
                            waits=(list(gate) if bb == 0 else []) +
                            estore[16 * jc:16 * (jc + 1)],
                            writes=[ebB])
                        S.op("vector", lambda e, bb=bb, eb=eb:
                             e.scalar_tensor_tensor(
                            xa[:, bb * T:(bb + 1) * T], eb[:, :],
                            carry_t[:, bb:bb + 1],
                            xa[:, bb * T:(bb + 1) * T], OP.mult, OP.add),
                            reads=[ebB, b_carry], writes=[b_xa])
                ts.append(t)
            return ts

        def s2_tasks(jc, gate, batch_x=False):
            xmst[jc] = []
            ts = f_stats_tasks(lambda bb: xa[:, bb * T:(bb + 1) * T], b_xa)
            if batch_x:
                def tx(jc=jc):
                    S.dmaop("scalar", lambda e, jc=jc: e.dma_start(
                        out=ffn[:, :].rearrange("p (c t) -> p c t", c=NBD),
                        in_=xs_d.rearrange("(q c) t -> q c t", c=NBD)
                            [:, :, jc * T:(jc + 1) * T]),
                        writes=[b_ffn])
                ts.append(tx)
            ts += f_rows_tasks()
            for k in range(8):
                def t(k=k, jc=jc, gate=gate):
                    for bb in (2 * k, 2 * k + 1):
                        rr, rb = rot[8 + bb % 2], b_rot[8 + bb % 2]
                        if batch_x:
                            xb, xbB = ffn[:, bb * T:(bb + 1) * T], b_ffn
                        else:
                            xb, xbB = rot[bb % 2], b_rot[bb % 2]
                            S.dmaop("scalar", lambda e, bb=bb, xb=xb, jc=jc:
                                    e.dma_start(
                                out=xb[:, :],
                                in_=xs_d.rearrange(
                                    "(q e) t -> q e t", e=NBD)
                                    [:, bb:bb + 1, jc * T:(jc + 1) * T]),
                                waits=(list(gate) if bb == 0 else []),
                                writes=[xbB])
                        S.op("vector", lambda e, bb=bb, rr=rr: e.tensor_sub(
                            rr[:, :], xa[:, bb * T:(bb + 1) * T],
                            psum[FBM][:, :]),
                            reads=[b_xa, b_ps[FBM]], writes=[rb])
                        S.op("vector", lambda e, rr=rr: e.tensor_mul(
                            rr[:, :], rr[:, :], psum[FBS][:, :]),
                            reads=[b_ps[FBS]], writes=[rb])
                        S.op("vector", lambda e, bb=bb, rr=rr, xb=xb,
                             xbB=xbB: e.tensor_add(
                            xa[:, bb * T:(bb + 1) * T], rr[:, :], xb[:, :]),
                            reads=[rb, xbB], writes=[b_xa])
                        xmst[jc].append(S.dmaop(
                            "gpsimd", lambda e, bb=bb, jc=jc: e.dma_start(
                                out=xm_d.ap().rearrange(
                                    "(q e) t -> q e t", e=NBD)
                                    [:, bb:bb + 1, jc * T:(jc + 1) * T],
                                in_=xa[:, bb * T:(bb + 1) * T]),
                            reads=[b_xa]))
                ts.append(t)
            return ts

        def s3_tasks(jc, gate):
            par = jc % 2
            ts = f_stats_tasks(lambda bb: xa[:, bb * T:(bb + 1) * T], b_xa)
            ts += f_rows_tasks()
            for k in range(8):
                def t(k=k, gate=gate, par=par):
                    for bb in (2 * k, 2 * k + 1):
                        rr, rb = rot[8 + bb % 2], b_rot[8 + bb % 2]
                        S.op("vector", lambda e, bb=bb, rr=rr: e.tensor_sub(
                            rr[:, :], xa[:, bb * T:(bb + 1) * T],
                            psum[FBM][:, :]),
                            reads=[b_xa, b_ps[FBM]], writes=[rb])
                        S.op("vector", lambda e, bb=bb, rr=rr, par=par:
                             e.tensor_mul(
                            ln2_p[par][:, bb * T:(bb + 1) * T], rr[:, :],
                            psum[FBS][:, :]),
                            waits=(list(gate) if bb == 0 and k == 0
                                   else []),
                            reads=[rb, b_ps[FBS]], writes=[b_ln2[par]])
                ts.append(t)
            return ts

        def ep_tasks(jc):
            ts = f_stats_tasks(lambda bb: xa[:, bb * T:(bb + 1) * T], b_xa)
            ts += f_rows_tasks()
            for k in range(8):
                def t(k=k, jc=jc):
                    for bb in (2 * k, 2 * k + 1):
                        rr, rb = rot[8 + bb % 2], b_rot[8 + bb % 2]
                        xm, xmB = rot[bb % 2], b_rot[bb % 2]
                        oo, ooB = rot[10 + bb % 2], b_rot[10 + bb % 2]
                        S.dmaop("scalar", lambda e, bb=bb, xm=xm, jc=jc:
                                e.dma_start(
                            out=xm[:, :],
                            in_=xm_d.ap().rearrange(
                                "(q e) t -> q e t", e=NBD)
                                [:, bb:bb + 1, jc * T:(jc + 1) * T]),
                            waits=xmst[jc], writes=[xmB])
                        S.op("vector", lambda e, bb=bb, rr=rr: e.tensor_sub(
                            rr[:, :], xa[:, bb * T:(bb + 1) * T],
                            psum[FBM][:, :]),
                            reads=[b_xa, b_ps[FBM]], writes=[rb])
                        S.op("vector", lambda e, rr=rr: e.tensor_mul(
                            rr[:, :], rr[:, :], psum[FBS][:, :]),
                            reads=[b_ps[FBS]], writes=[rb])
                        S.op("vector", lambda e, rr=rr, xm=xm, oo=oo:
                             e.tensor_add(oo[:, :], rr[:, :], xm[:, :]),
                            reads=[rb, xmB], writes=[ooB])
                        S.dmaop("gpsimd", lambda e, bb=bb, jc=jc, oo=oo:
                                e.dma_start(
                            out=out_d.rearrange("(q e) t -> q e t", e=NBD)
                                [:, bb:bb + 1, jc * T:(jc + 1) * T],
                            in_=oo[:, :]), reads=[ooB])
                ts.append(t)
            return ts

        for t in (s1_tasks(0, ffn_gate, batch_e=True) +
                  s2_tasks(0, ffn_gate, batch_x=True) +
                  s3_tasks(0, ffn_gate)):
            t()

        for j in range(NCH):
            tasks = []
            if j > 0:
                tasks += ep_tasks(j - 1)
            if j + 1 < NCH:
                tasks += s1_tasks(j + 1, [])
                tasks += s2_tasks(j + 1, [])
                tasks += s3_tasks(j + 1, ffn_gate if j == 0 else [])
            ti = 0
            ln2c, ln2cB = ln2_p[j % 2], b_ln2[j % 2]
            def sumsq_mm(bb):
                sq, sqB = sqg[bb % 2], b_sqg[bb % 2]
                first, last = (bb == 0), (bb == NBH - 1)
                bw = ([b_ps[4].w] if first and b_ps[4].w else []) + \
                    (list(b_ps[4].r) if first else [])
                S.op("tensor", lambda e, sq=sq, first=first, last=last:
                     e.matmul(psum[4][0:1, :], onesb[:, 0:1], sq[:, :],
                              start=first, stop=last),
                     waits=bw, reads=[sqB],
                     writes=[b_ps[4]] if last else [])
                if not last:
                    b_ps[4].w = S.tok("pe")

            for beta in range(NBH):
                if beta >= 2:
                    sumsq_mm(beta - 2)
                if ti < len(tasks):
                    tasks[ti]()
                    ti += 1
                wg_i = 2 * (beta % 3)
                wu_i = wg_i + 1
                gb_i = beta % 2
                ub_i = 2 if beta % 2 == 0 else 7
                S.dmaop("sync", lambda e, beta=beta, wg_i=wg_i:
                        e.dma_start(
                    out=wt[wg_i][:, :],
                    in_=gq_d.ap()[:, 2048 * beta:2048 * (beta + 1)]),
                    waits=[t_ag2], writes=[b_wt[wg_i]])
                S.dmaop("sync", lambda e, beta=beta, wu_i=wu_i:
                        e.dma_start(
                    out=wt[wu_i][:, :],
                    in_=gq_d.ap()[:, 2048 * (64 + beta):2048 * (65 + beta)]),
                    waits=[t_ag2], writes=[b_wt[wu_i]])
                for bank_i, wi in ((gb_i, wg_i), (ub_i, wu_i)):
                    mm_group(bank_i,
                             (lambda wi: lambda k:
                              wt[wi][:, k * 128:(k + 1) * 128])(wi),
                             (lambda l2: lambda k:
                              l2[:, k * T:(k + 1) * T])(ln2c),
                             [b_wt[wi], ln2cB], NBD)
                tg, tgB = rot[6 + beta % 2], b_rot[6 + beta % 2]
                sg, sgB = rot[4 + beta % 2], b_rot[4 + beta % 2]
                tt, ttB = rot[2 + beta % 2], b_rot[2 + beta % 2]
                S.op("scalar", lambda e, gb_i=gb_i, tg=tg: e.activation(
                    tg[:, :], psum[gb_i][:, :], AF.Tanh, scale=0.5),
                    reads=[b_ps[gb_i]], writes=[tgB])
                S.op("vector", lambda e, tg=tg, sg=sg: e.tensor_scalar(
                    sg[:, :], tg[:, :], 0.5, 0.5, OP.mult, OP.add),
                    reads=[tgB], writes=[sgB])
                S.op("vector", lambda e, gb_i=gb_i, sg=sg, tt=tt:
                     e.tensor_mul(tt[:, :], psum[gb_i][:, :], sg[:, :]),
                     reads=[sgB, b_ps[gb_i]], writes=[ttB])
                hb_t = hb[beta // 16]
                hb_sl = slice((beta % 16) * T, (beta % 16 + 1) * T)
                hbB = b_hb[beta // 16]
                S.op("vector", lambda e, ub_i=ub_i, tt=tt, hb_t=hb_t,
                     hb_sl=hb_sl: e.tensor_mul(
                    hb_t[:, hb_sl], tt[:, :], psum[ub_i][:, :]),
                    reads=[ttB, b_ps[ub_i]], writes=[hbB])
                sq, sqB = sqg[beta % 2], b_sqg[beta % 2]
                S.op("gpsimd", lambda e, sq=sq, hb_t=hb_t, hb_sl=hb_sl:
                     e.tensor_mul(sq[:, :], hb_t[:, hb_sl], hb_t[:, hb_sl]),
                     reads=[hbB], writes=[sqB])
            sumsq_mm(NBH - 2)
            sumsq_mm(NBH - 1)
            while ti < len(tasks):
                tasks[ti]()
                ti += 1
            S.op("scalar", lambda e: e.activation(
                rowC[0:1, :], psum[4][0:1, :], AF.Ln,
                bias=cst[0:1, 1:2], scale=1.0 / HF),
                waits=[t_const], reads=[b_ps[4]], writes=[b_rows])
            S.op("scalar", lambda e: e.activation(
                rowB[0:1, :], rowC[0:1, :], AF.Exp, scale=-0.5),
                reads=[b_rows], writes=[b_rows])
            bcast(FBS, rowB)
            S.op("vector", lambda e: e.tensor_copy(
                invr_t[:, :], psum[FBS][:, :]), reads=[b_ps[FBS]],
                writes=[b_invr])
            for hp in range(2):
                for beta in range(NBH):
                    wq_i = beta % 3
                    S.dmaop("sync", lambda e, beta=beta, hp=hp, wq_i=wq_i:
                            e.dma_start(
                        out=wqp[wq_i][:, :],
                        in_=pq_d.ap()[beta * 128:(beta + 1) * 128,
                                      hp * 1024:(hp + 1) * 1024]),
                        waits=[t_ag3], writes=[b_wqp[wq_i]])
                    hb_t = hb[beta // 16]
                    hb_sl = slice((beta % 16) * T, (beta % 16 + 1) * T)
                    hbB = b_hb[beta // 16]
                    for dk in range(8):
                        first = (beta == 0)
                        lastg = (beta == NBH - 1)
                        tr = (dk == 7) or lastg
                        waits = []
                        if first:
                            if b_ps[dk].w:
                                waits.append(b_ps[dk].w)
                            waits.extend(b_ps[dk].r)
                        if dk == 0:
                            if b_wqp[wq_i].w:
                                waits.append(b_wqp[wq_i].w)
                            if hbB.w:
                                waits.append(hbB.w)
                        S.op("tensor", lambda e, dk=dk, wq_i=wq_i,
                             hb_t=hb_t, hb_sl=hb_sl, first=first,
                             lastg=lastg: e.matmul(
                            psum[dk][:, :],
                            wqp[wq_i][:, dk * 128:(dk + 1) * 128],
                            hb_t[:, hb_sl], start=first, stop=lastg),
                            waits=waits,
                            reads=[b_wqp[wq_i], hbB] if dk == 7 else [],
                            writes=[b_ps[dk]] if lastg else [],
                            track=tr)
                        if not lastg:
                            b_ps[dk].w = S.tok("pe")
                for dk in range(8):
                    db = hp * 8 + dk
                    S.op("vector", lambda e, dk=dk, db=db: e.tensor_mul(
                        xa[:, db * T:(db + 1) * T], psum[dk][:, :],
                        invr_t[:, :]),
                        reads=[b_ps[dk], b_invr], writes=[b_xa])

        for t in ep_tasks(NCH - 1):
            t()

        S.op("sync", lambda e: None,
             waits=S.qtoks("qs") + S.qtoks("qc") + S.qtoks("qg"),
             track=False)

        with nc.Block() as block:
            @block.sync
            def _(eng):
                S.replay("sync", eng, sems)

            @block.scalar
            def _(eng):
                S.replay("scalar", eng, sems)

            @block.vector
            def _(eng):
                S.replay("vector", eng, sems)

            @block.gpsimd
            def _(eng):
                S.replay("gpsimd", eng, sems)

            @block.tensor
            def _(eng):
                S.replay("tensor", eng, sems)

    return nc


_NC_CACHE = {}


def kernel(L0_attn_x, L0_attn_W_in, L0_attn_a_param, L0_ffn_w_gate_up,
           L0_ffn_w_proj, debug=False):
    xT = np.ascontiguousarray(np.transpose(np.asarray(L0_attn_x), (0, 2, 1)))
    W_inT = np.asarray(L0_attn_W_in).T
    W2 = np.ascontiguousarray(
        W_inT.reshape(16, 128, 32, 128).transpose(1, 2, 0, 3))
    wguT = np.ascontiguousarray(np.asarray(L0_ffn_w_gate_up).T)
    wpT = np.asarray(L0_ffn_w_proj).T
    wpT2 = np.ascontiguousarray(
        wpT.reshape(HF, 128, 16).transpose(0, 2, 1).reshape(HF, DM))
    apar_h = np.ascontiguousarray(
        np.asarray(L0_attn_a_param, np.float32).reshape(128, 1))

    expq_h = np.zeros((2, 128), ml_dtypes.bfloat16)
    expq_h[0, 0:64] = 1
    expq_h[1, 64:128] = 1
    in_maps = []
    for c in range(NCORES):
        b, half = c // 2, c % 2
        sel = np.zeros((8, 1), np.float32)
        if half == 1:
            sel[c - 1, 0] = 1.0
        in_maps.append({
            "xs": np.ascontiguousarray(xT[b, :, half * TOK:(half + 1) * TOK]),
            "wslab": np.ascontiguousarray(
                W2[16 * c:16 * (c + 1)].reshape(16, 16 * FD)),
            "gslab": np.ascontiguousarray(wguT[256 * c:256 * (c + 1)]),
            "pslab": np.ascontiguousarray(wpT2[1024 * c:1024 * (c + 1)]),
            "apar": apar_h,
            "sel": sel,
            "expq_in": expq_h,
        })
    key = ("main", debug)
    if key not in _NC_CACHE:
        _NC_CACHE[key] = build(debug=debug)
    nc = _NC_CACHE[key]
    res = run_bass_kernel_spmd(nc, in_maps, core_ids=list(range(NCORES)))
    out = np.empty((BSZ, SEQ, DM), np.float32)
    for c in range(NCORES):
        b, half = c // 2, c % 2
        out[b, half * TOK:(half + 1) * TOK, :] = res.results[c]["outT"].T
    if debug:
        return out, res
    return out




# version 46 (final)
# speedup vs baseline: 1.0621x; 1.0621x over previous
import numpy as np
import ml_dtypes
import concourse.bass as bass
import concourse.mybir as mybir
from concourse.bass_utils import run_bass_kernel_spmd

F32 = mybir.dt.float32
BF16 = mybir.dt.bfloat16
AF = mybir.ActivationFunctionType
OP = mybir.AluOpType

BSZ, SEQ, DM = 4, 4096, 2048
FD = 2 * DM
HF = 4 * DM
TOK = 2048
T = 512
NCH = TOK // T
NBD = DM // 128
NBH = HF // 128
CQ = 0.50048828125
EPS_LN = 1e-5
EPS_RMS = 1.1920929e-7
NCORES = 8

ENGS = ("sync", "scalar", "vector", "gpsimd", "tensor")
QUEUE_SEM = {"sync": "qs", "scalar": "qc", "gpsimd": "qg"}
ENG_SEM = {"vector": "dv", "scalar": "ac", "gpsimd": "po", "tensor": "pe",
           "sync": "qs"}


class Buf:
    __slots__ = ("name", "w", "r")

    def __init__(self, name):
        self.name = name
        self.w = None
        self.r = []


NDMA = 8


class Sched:
    def __init__(self):
        self.ops = {e: [] for e in ENGS}
        self.cnt = {}
        self.rr = {}

    def _bump(self, sem, step):
        self.cnt[sem] = self.cnt.get(sem, 0) + step
        return (sem, self.cnt[sem])

    def op(self, eng, fn, reads=(), writes=(), waits=(), sem=None,
           dma=False, track=True):
        wl = list(waits)
        for b in reads:
            if b.w is not None:
                wl.append(b.w)
        for b in writes:
            if b.w is not None:
                wl.append(b.w)
            wl.extend(b.r)
        if sem is None:
            if dma:
                q = QUEUE_SEM[eng]
                i = self.rr.get(q, 0)
                self.rr[q] = (i + 1) % NDMA
                sem = f"{q}{i}"
                wl.append((sem, self.cnt.get(sem, 0)))
            else:
                sem = ENG_SEM[eng]
        step = 16 if dma else 1
        tok = self._bump(sem, step) if track else None
        self.ops[eng].append((tuple(wl), fn, sem if track else None, step))
        if track:
            for b in reads:
                b.r.append(tok)
            for b in writes:
                b.w = tok
                b.r = []
        return tok

    def dmaop(self, eng, fn, reads=(), writes=(), waits=()):
        return self.op(eng, fn, reads, writes, waits, dma=True)

    def tok(self, sem):
        return (sem, self.cnt.get(sem, 0))

    def qtoks(self, q):
        return [(f"{q}{i}", self.cnt.get(f"{q}{i}", 0)) for i in range(NDMA)]

    def replay(self, ename, eng, sems):
        seen = {}
        for waits, fn, sem, step in self.ops[ename]:
            for (s, v) in waits:
                if v > 0 and seen.get(s, 0) < v:
                    eng.wait_ge(sems[s], v)
                    seen[s] = v
            instr = fn(eng)
            if sem is not None:
                instr.then_inc(sems[sem], step)


def build(debug=False):
    nc = bass.Bass("TRN2", target_bir_lowering=False, debug=False)

    xs_d = nc.dram_tensor("xs", [DM, TOK], F32, kind="ExternalInput").ap()
    wslab = nc.dram_tensor("wslab", [16, FD * 16], F32,
                           kind="ExternalInput").ap()
    gslab = nc.dram_tensor("gslab", [256, 2 * HF], F32,
                           kind="ExternalInput").ap()
    pslab = nc.dram_tensor("pslab", [1024, DM], F32, kind="ExternalInput").ap()
    apar = nc.dram_tensor("apar", [128, 1], F32, kind="ExternalInput").ap()
    sel_d = nc.dram_tensor("sel", [8, 1], F32, kind="ExternalInput").ap()
    expq_d = nc.dram_tensor("expq_in", [2, 128], BF16,
                            kind="ExternalInput").ap()
    out_d = nc.dram_tensor("outT", [DM, TOK], F32, kind="ExternalOutput").ap()

    wc_d = nc.dram_tensor("wc", [16, FD * 16], BF16)
    wg_d = nc.dram_tensor("wg", [128, FD * 16], BF16, addr_space="Shared")
    gc_d = nc.dram_tensor("gc", [16, 2 * HF * 16], BF16)
    gq_d = nc.dram_tensor("gq", [128, 2 * HF * 16], BF16,
                          addr_space="Shared")
    pc_d = nc.dram_tensor("pc", [1024, DM], BF16)
    pq_d = nc.dram_tensor("pq", [HF, DM], BF16, addr_space="Shared")
    srg_d = nc.dram_tensor("srg", [4, 2 * HF], BF16)
    srp_d = nc.dram_tensor("srp", [16, DM], BF16)
    carc_d = nc.dram_tensor("carc", [128, 16], F32)
    carg_d = nc.dram_tensor("carg", [8 * 128, 16], F32, addr_space="Shared")
    car1_d = nc.dram_tensor("car1", [1, DM], F32)
    kind_scr = "ExternalOutput" if debug else "Internal"
    h_d = nc.dram_tensor("hscr", [DM, TOK], F32, kind=kind_scr)
    e_d = nc.dram_tensor("escr", [DM, TOK], F32, kind=kind_scr)
    xm_d = nc.dram_tensor("xmscr", [DM, TOK], F32, kind=kind_scr)
    mrows_d = nc.dram_tensor("mrows", [8, T], F32)

    from contextlib import ExitStack
    with ExitStack() as stk:
        def sb(name, shape, dtype):
            return stk.enter_context(nc.sbuf_tensor(name, shape, dtype))

        apar_t = sb("apar_t", [128, 1], F32)
        sneg = sb("sneg", [128, 1], F32)
        hsneg = sb("hsneg", [128, 1], F32)
        selt = sb("selt", [8, 1], F32)
        carry_t = sb("carry_t", [128, 16], F32)
        zeros_t = sb("zeros_t", [128, T], F32)
        ones1 = sb("ones1", [1, 128], F32)
        onesb = sb("onesb", [128, 1], BF16)
        onesf = sb("onesf", [128, 1], F32)
        blkq = sb("blkq", [128, 2], BF16)
        expq = sb("expq", [2, 128], BF16)
        invr_t = sb("invr_t", [128, T], F32)
        rowA = sb("rowA", [1, T], F32)
        rowB = sb("rowB", [1, T], F32)
        rowC = sb("rowC", [1, T], F32)
        srow_t = sb("srow_t", [2, 4096], BF16)
        cst = sb("cst", [128, 2], F32)
        carh_t = [sb(f"carh{i}", [128, 16], F32) for i in range(2)]
        carC_t = [sb(f"carC{i}", [128, 16], F32) for i in range(2)]
        cart_t = sb("cart", [128, 16], F32)

        xa = sb("xa", [128, NBD * T], F32)
        ln1 = sb("ln1", [128, NBD * T], BF16)
        wt = [sb(f"wt{i}", [128, 2048], BF16) for i in range(6)]
        bxs = sb("bxs", [128, NBD * T], BF16)
        abf = sb("abf", [128, NBD * T], BF16)
        ybf = sb("ybf", [128, NBD * T], BF16)
        NROT = 13
        rot = [sb(f"rot{i}", [128, T], F32) for i in range(NROT)]
        brot = [sb(f"brot{i}", [128, T], BF16) for i in range(4)]

        ffn = sb("ffn", [128, NBD * T], F32)
        wqp = [sb(f"wqp{i}", [128, 1024], BF16) for i in range(3)]
        sqg = [zeros_t[:, 0:256].bitcast(BF16),
               zeros_t[:, 256:512].bitcast(BF16)]

        hb = [ln1, abf, ybf, bxs]
        agc_t = xa[0:8, 0:DM]
        crow = ln1[0:1, 0:2 * DM].bitcast(F32)
        ln2_p = [ffn[:, 0:4096].bitcast(BF16),
                 ffn[:, 4096:8192].bitcast(BF16)]
        q_wbf_ap = [ffn[:, 4096 + i * 1024:4096 + (i + 1) * 1024].bitcast(BF16)
                    for i in range(2)]
        q_A_ap2 = [ffn[:, 6144:7168].bitcast(BF16),
                   ffn[:, 1024:2048].bitcast(BF16)]
        q_B_ap2 = [ffn[:, 7168:8192].bitcast(BF16),
                   ffn[:, 2048:3072].bitcast(BF16)]
        q_C_ap2 = [ffn[:, 0:1024].bitcast(BF16),
                   ffn[:, 3072:4096].bitcast(BF16)]

        psum = [stk.enter_context(nc.psum_tensor(f"pb{i}", [128, 512], F32))
                for i in range(8)]

        sem_names = (["pe", "dv", "ac", "po", "cc"] +
                     [f"{q}{i}" for q in ("qs", "qc", "qg")
                      for i in range(NDMA)])
        sems = {n: stk.enter_context(nc.semaphore(n)) for n in sem_names}
        S = Sched()

        b_rot = [Buf(f"rot{i}") for i in range(NROT)]
        b_brot = [Buf(f"brot{i}") for i in range(4)]
        b_xa = Buf("xa")
        b_ln1 = Buf("ln1")
        b_wt = [Buf(f"wt{i}") for i in range(6)]
        b_bxs = Buf("bxs")
        b_abf = Buf("abf")
        b_ybf = Buf("ybf")
        b_ffn = Buf("ffn")
        b_ln2 = [Buf("ln2a"), Buf("ln2b")]
        b_sqg = [Buf("sqg0"), Buf("sqg1")]
        b_wqp = [Buf(f"wqp{i}") for i in range(3)]
        b_ps = [Buf(f"ps{i}") for i in range(8)]
        b_hb = [b_ln1, b_abf, b_ybf, b_bxs]
        b_rows = Buf("rows")
        b_srow = Buf("srow")
        b_carh = [Buf(f"carh{i}") for i in range(2)]
        b_carC = [Buf(f"carC{i}") for i in range(2)]
        b_cart = Buf("cart")
        b_carry = Buf("carry")
        b_invr = Buf("invr")
        b_misc = Buf("misc")
        b_qw = [Buf("qw0"), Buf("qw1")]
        b_qA2 = [Buf("qA0"), Buf("qA1")]
        b_qB2 = [Buf("qB0"), Buf("qB1")]
        b_qC2 = [Buf("qC0"), Buf("qC1")]
        b_srow2 = [b_srow, Buf("srow1")]

        t_sel = S.dmaop("sync", lambda e: e.dma_start(out=selt[:, :],
                                                      in_=sel_d))
        t_ap = S.dmaop("sync", lambda e: e.dma_start(out=apar_t[:, :],
                                                     in_=apar))
        S.op("vector", lambda e: e.memset(zeros_t[:, :], 0.0),
             writes=[b_misc])
        S.op("vector", lambda e: e.memset(ones1[:, :], 1.0), writes=[b_misc])
        S.op("vector", lambda e: e.memset(onesb[:, :], 1.0), writes=[b_misc])
        S.op("vector", lambda e: e.memset(onesf[:, :], 1.0), writes=[b_misc])
        S.op("vector", lambda e: e.memset(blkq[:, :], 0.0), writes=[b_misc])
        S.op("vector", lambda e: e.memset(blkq[0:64, 0:1], 1.0),
             writes=[b_misc])
        S.dmaop("sync", lambda e: e.dma_start(out=expq[:, :], in_=expq_d))
        S.op("vector", lambda e: e.memset(blkq[64:128, 1:2], 1.0),
             writes=[b_misc])
        S.op("vector", lambda e: e.memset(cst[:, 0:1], EPS_LN),
             writes=[b_misc])
        t_const = S.op("vector", lambda e: e.memset(cst[:, 1:2], EPS_RMS),
                       writes=[b_misc])

        S.op("vector", lambda e: e.scalar_tensor_tensor(
            rot[0][:, 0:1], apar_t[:, 0:1], -1.0, apar_t[:, 0:1],
            OP.mult, OP.max), waits=[t_ap], writes=[b_rot[0]])
        S.op("scalar", lambda e: e.activation(
            rot[1][:, 0:1], rot[0][:, 0:1], AF.Exp, scale=-1.0),
            reads=[b_rot[0]], writes=[b_rot[1]])
        S.op("scalar", lambda e: e.activation(
            rot[2][:, 0:1], rot[1][:, 0:1], AF.Ln, bias=1.0),
            reads=[b_rot[1]], writes=[b_rot[2]])
        S.op("vector", lambda e: e.tensor_scalar_max(
            rot[3][:, 0:1], apar_t[:, 0:1], 0.0), waits=[t_ap],
            writes=[b_rot[3]])
        S.op("vector", lambda e: e.tensor_add(
            rot[3][:, 0:1], rot[3][:, 0:1], rot[2][:, 0:1]),
            reads=[b_rot[2]], writes=[b_rot[3]])
        S.op("vector", lambda e: e.tensor_scalar_mul(
            sneg[:, 0:1], rot[3][:, 0:1], -8.0), reads=[b_rot[3]],
            writes=[b_misc])
        t_sneg = S.op("vector", lambda e: e.tensor_scalar_mul(
            hsneg[:, 0:1], rot[3][:, 0:1], -4.0), reads=[b_rot[3]],
            writes=[b_misc])

        t_wc = S.dmaop("gpsimd", lambda e: e.dma_start(out=wc_d.ap(),
                                                       in_=wslab))
        t_ag1 = S.op("gpsimd", lambda e: e.collective_compute(
            "AllGather", OP.bypass, replica_groups=[list(range(NCORES))],
            ins=[wc_d.ap()], outs=[wg_d.ap()]), waits=[t_wc], sem="cc")

        QW = 2048
        qpieces = []
        for pbk in range(2):
            for cs in range(2 * HF // QW):
                qpieces.append(("g", pbk, cs, gslab, gc_d, srg_d))
        for pbk in range(8):
            qpieces.append(("p", pbk, 0, pslab, pc_d, srp_d))
        gstores, pstores = [], []

        def qctx(qis):
            ctx = []
            for qi in qis:
                kind, pbk, cs, src_d, dst_d, sr_d = qpieces[qi]
                par = qi % 2
                ctx.append(dict(
                    kind=kind, pbk=pbk, cs=cs,
                    r0=pbk * 128, c0=cs * QW, src_d=src_d,
                    dst_d=dst_d, wb=q_wbf_ap[par], wbB=b_qw[par],
                    qA=q_A_ap2[par], qAb=b_qA2[par],
                    qB=q_B_ap2[par], qBb=b_qB2[par],
                    qC=q_C_ap2[par], qCb=b_qC2[par],
                    sof=par * 2048, srb=b_srow2[par]))
            return ctx

        def emit_qstep(ctx, step):
            if step == 0:
                for c in ctx:
                    S.dmaop("gpsimd", lambda e, c=c: e.dma_start(
                        out=c["wb"], in_=c["src_d"][c["r0"]:c["r0"] + 128,
                                                   c["c0"]:c["c0"] + QW]),
                        writes=[c["wbB"]])
            elif step == 1:
                for ci, c in enumerate(ctx):
                    if ci == 0:
                        S.op("vector", lambda e, c=c: e.scalar_tensor_tensor(
                            c["qA"], c["wb"], -1.0, c["wb"],
                            OP.mult, OP.max),
                            reads=[c["wbB"]], writes=[c["qAb"]])
                    else:
                        S.op("scalar", lambda e, c=c: e.activation(
                            c["qA"], c["wb"], AF.Abs),
                            reads=[c["wbB"]], writes=[c["qAb"]])
            elif step == 2:
                for ci, c in enumerate(ctx):
                    for k in range(QW // 512):
                        pb = 5 + (2 * ci + k) % 2
                        S.op("tensor", lambda e, c=c, k=k, pb=pb: e.matmul(
                            psum[pb][0:2, :], blkq[:, :],
                            c["qA"][:, k * 512:(k + 1) * 512],
                            start=True, stop=True), waits=[t_const],
                            reads=[c["qAb"]], writes=[b_ps[pb]])
                        if k % 2 == 0:
                            S.op("vector", lambda e, c=c, k=k, pb=pb:
                                 e.tensor_scalar(
                                srow_t[0:2, c["sof"] + k * 512:
                                       c["sof"] + (k + 1) * 512],
                                psum[pb][0:2, :], 1.0 / 64.0, 1e-8,
                                OP.mult, OP.max),
                                reads=[b_ps[pb]], writes=[c["srb"]])
                        else:
                            S.op("scalar", lambda e, c=c, k=k, pb=pb:
                                 e.activation(
                                srow_t[0:2, c["sof"] + k * 512:
                                       c["sof"] + (k + 1) * 512],
                                psum[pb][0:2, :], AF.Copy,
                                scale=1.0 / 64.0),
                                reads=[b_ps[pb]], writes=[c["srb"]])
            elif step == 3:
                for ci, c in enumerate(ctx):
                    for k in range(QW // 512):
                        pb = 5 + (2 * ci + k) % 2
                        S.op("tensor", lambda e, c=c, k=k, pb=pb: e.matmul(
                            psum[pb][:, :], expq[:, :],
                            srow_t[0:2, c["sof"] + k * 512:
                                   c["sof"] + (k + 1) * 512],
                            start=True, stop=True), waits=[t_const],
                            reads=[c["srb"]], writes=[b_ps[pb]])
                        if k % 2 == 0:
                            S.op("vector", lambda e, c=c, k=k, pb=pb:
                                 e.tensor_copy(
                                c["qC"][:, k * 512:(k + 1) * 512],
                                psum[pb][:, :]),
                                reads=[b_ps[pb]], writes=[c["qCb"]])
                        else:
                            S.op("scalar", lambda e, c=c, k=k, pb=pb:
                                 e.activation(
                                c["qC"][:, k * 512:(k + 1) * 512],
                                psum[pb][:, :], AF.Copy),
                                reads=[b_ps[pb]], writes=[c["qCb"]])
            elif step == 4:
                for c in ctx:
                    S.op("vector", lambda e, c=c: e.scalar_tensor_tensor(
                        c["qB"], c["qC"], CQ, c["qA"], OP.mult, OP.is_lt),
                        reads=[c["qCb"], c["qAb"]], writes=[c["qBb"]])
            elif step == 5:
                for c in ctx:
                    S.op("scalar", lambda e, c=c: e.activation(
                        c["qA"], c["wb"], AF.Sign),
                        reads=[c["wbB"]], writes=[c["qAb"]])
                for c in ctx:
                    S.op("vector", lambda e, c=c: e.tensor_mul(
                        c["qB"], c["qB"], c["qC"]),
                        reads=[c["qCb"]], writes=[c["qBb"]])
            elif step == 6:
                for c in ctx:
                    S.op("vector", lambda e, c=c: e.tensor_mul(
                        c["qB"], c["qB"], c["qA"]),
                        reads=[c["qAb"]], writes=[c["qBb"]])
            elif step == 7:
                for c in ctx:
                    if c["kind"] == "g":
                        for bl in range(16):
                            beta = c["cs"] * 16 + bl
                            t_st = S.dmaop(
                                "scalar", lambda e, c=c, bl=bl, beta=beta:
                                e.dma_start(
                                    out=c["dst_d"].ap()
                                    [c["pbk"] * 8:(c["pbk"] + 1) * 8,
                                     beta * 2048:(beta + 1) * 2048],
                                    in_=c["qB"][:, bl * 128:(bl + 1) * 128]),
                                reads=[c["qBb"]])
                            gstores.append(t_st)
                    else:
                        t_st = S.dmaop("scalar", lambda e, c=c: e.dma_start(
                            out=c["dst_d"].ap()[c["r0"]:c["r0"] + 128,
                                                c["c0"]:c["c0"] + QW],
                            in_=c["qB"]), reads=[c["qBb"]])
                        pstores.append(t_st)

        def emit_ag2():
            return S.op("gpsimd", lambda e: e.collective_compute(
                "AllGather", OP.bypass, replica_groups=[list(range(NCORES))],
                ins=[gc_d.ap()], outs=[gq_d.ap()]), waits=gstores, sem="cc")

        def emit_ag3():
            return S.op("gpsimd", lambda e: e.collective_compute(
                "AllGather", OP.bypass, replica_groups=[list(range(NCORES))],
                ins=[pc_d.ap()], outs=[pq_d.ap()]), waits=pstores, sem="cc")

        def stats_cpsq(src, src_buf, b, eng="gpsimd"):
            sq, sqB = brot[b % 4], b_brot[b % 4]
            S.op(eng, lambda e, src=src, sq=sq: e.tensor_mul(
                sq[:, :], src, src), reads=[src_buf], writes=[sqB])

        def stats_mm(src, src_buf, b, first, last, bs=3, bq=4):
            sq, sqB = brot[b % 4], b_brot[b % 4]
            bw3 = ([b_ps[bs].w] if first and b_ps[bs].w else []) + \
                  (list(b_ps[bs].r) if first else [])
            bw4 = ([b_ps[bq].w] if first and b_ps[bq].w else []) + \
                  (list(b_ps[bq].r) if first else [])
            S.op("tensor", lambda e, src=src, first=first, last=last, bs=bs:
                 e.matmul(psum[bs][0:1, :], onesf[:, 0:1], src,
                          start=first, stop=last),
                 waits=[t_const] + bw3, reads=[src_buf],
                 writes=[b_ps[bs]] if last else [])
            S.op("tensor", lambda e, sq=sq, first=first, last=last, bq=bq:
                 e.matmul(psum[bq][0:1, :], onesb[:, 0:1], sq[:, :],
                          start=first, stop=last),
                 waits=bw4, reads=[sqB],
                 writes=[b_ps[bq]] if last else [])
            if not last:
                b_ps[bs].w = S.tok("pe")
                b_ps[bq].w = S.tok("pe")

        def stats_block(src, src_buf, b, first, last, eng="gpsimd"):
            stats_cpsq(src, src_buf, b, eng=eng)
            stats_mm(src, src_buf, b, first, last)

        def stats(src_ap_fn, src_buf, n_blocks, eng="gpsimd"):
            for b in range(n_blocks):
                stats_block(src_ap_fn(b), src_buf, b,
                            b == 0, b == n_blocks - 1, eng=eng)

        def ln_rows(n, eps_i=0, bs=3, bq=4):
            S.op("vector", lambda e, bs=bs: e.tensor_scalar_mul(
                rowA[0:1, :], psum[bs][0:1, :], 1.0 / n),
                reads=[b_ps[bs]], writes=[b_rows])
            S.op("vector", lambda e, bq=bq: e.tensor_scalar_mul(
                rowB[0:1, :], psum[bq][0:1, :], 1.0 / n),
                reads=[b_ps[bq]], writes=[b_rows])
            S.op("vector", lambda e: e.tensor_mul(
                rowC[0:1, :], rowA[0:1, :], rowA[0:1, :]), writes=[b_rows])
            S.op("vector", lambda e: e.tensor_sub(
                rowB[0:1, :], rowB[0:1, :], rowC[0:1, :]), writes=[b_rows])
            S.op("scalar", lambda e, eps_i=eps_i: e.activation(
                rowC[0:1, :], rowB[0:1, :], AF.Ln, bias=cst[0:1, eps_i:eps_i + 1]),
                waits=[t_const], reads=[b_rows], writes=[b_rows])
            return S.op("scalar", lambda e: e.activation(
                rowB[0:1, :], rowC[0:1, :], AF.Exp, scale=-0.5),
                reads=[b_rows], writes=[b_rows])

        def bcast(dst_i, row_t):
            return S.op("tensor", lambda e, dst_i=dst_i, row_t=row_t:
                        e.matmul(
                psum[dst_i][:, :], ones1[0:1, :], row_t[0:1, :],
                start=True, stop=True), reads=[b_rows],
                writes=[b_ps[dst_i]], waits=[t_const])

        def mm_group(bank_i, lhsT_fn, rhs_fn, rbufs, nk, first_grp=True,
                     last_grp=True, extra_first=()):
            for k in range(nk):
                first, last = (k == 0 and first_grp), \
                    (k == nk - 1 and last_grp)
                waits = list(extra_first) if k == 0 else []
                if first:
                    if b_ps[bank_i].w:
                        waits.append(b_ps[bank_i].w)
                    waits.extend(b_ps[bank_i].r)
                    for rb in rbufs:
                        if rb.w:
                            waits.append(rb.w)
                S.op("tensor", lambda e, k=k: e.matmul(
                    psum[bank_i][:, :], lhsT_fn(k), rhs_fn(k),
                    start=(k == 0 and first_grp),
                    stop=(k == nk - 1 and last_grp)),
                    waits=waits, reads=list(rbufs) if last else [],
                    writes=[b_ps[bank_i]] if last else [],
                    track=last)
                if not last:
                    b_ps[bank_i].w = S.tok("pe")

        hstore, estore = [], []

        def xa_load(j, src, src_waits, qlay=False):
            pat = "(q e) s -> q e s" if qlay else "(n p) s -> p n s"
            kw = {"e": NBD} if qlay else {"p": 128}
            return S.dmaop("scalar", lambda e, j=j, src=src: e.dma_start(
                out=xa[:, :].rearrange("p (n t) -> p n t", n=NBD),
                in_=src.rearrange(pat, **kw)[:, :, j * T:(j + 1) * T]),
                waits=src_waits, writes=[b_xa])

        mrow_st = {}

        def ln1_apply(j):
            S.dmaop("sync", lambda e, j=j: e.dma_start(
                out=rowA[0:1, :], in_=mrows_d.ap()[2 * j:2 * j + 1, :]),
                waits=mrow_st[j], writes=[b_rows])
            S.dmaop("sync", lambda e, j=j: e.dma_start(
                out=rowB[0:1, :], in_=mrows_d.ap()[2 * j + 1:2 * j + 2, :]),
                waits=mrow_st[j], writes=[b_rows])
            bcast(7, rowA)
            bcast(4, rowB)
            for b in range(NBD):
                rr, rb = rot[b % 2], b_rot[b % 2]
                S.op("vector", lambda e, b=b, rr=rr: e.tensor_sub(
                    rr[:, :], xa[:, b * T:(b + 1) * T], psum[7][:, :]),
                    reads=[b_xa, b_ps[7]], writes=[rb])
                S.op("vector", lambda e, b=b, rr=rr: e.tensor_mul(
                    ln1[:, b * T:(b + 1) * T], rr[:, :], psum[4][:, :]),
                    reads=[rb, b_ps[4]], writes=[b_ln1])

        for jj in (1, 2, 3, 0):
            xa_load(jj, xs_d, [])
            stats(lambda b: xa[:, b * T:(b + 1) * T], b_xa, NBD,
                  eng="vector")
            ln_rows(DM)
            mrow_st[jj] = [
                S.dmaop("gpsimd", lambda e, jj=jj: e.dma_start(
                    out=mrows_d.ap()[2 * jj:2 * jj + 1, :],
                    in_=rowA[0:1, :]), reads=[b_rows]),
                S.dmaop("gpsimd", lambda e, jj=jj: e.dma_start(
                    out=mrows_d.ap()[2 * jj + 1:2 * jj + 2, :],
                    in_=rowB[0:1, :]), reads=[b_rows])]
        ln1_apply(0)
        xa_load(1, xs_d, [])
        for j in range(NCH):
            qpp0 = qctx([6 * j, 6 * j + 1])
            qpp1 = qctx([6 * j + 2, 6 * j + 3])
            qpp2 = qctx([6 * j + 4, 6 * j + 5])
            for b in range(NBD):
                if b == 0:
                    gx_i, ga_i = 5, 6
                elif b == 1:
                    gx_i, ga_i = 4, 7
                else:
                    gx_i = (2 * b) % 4
                    ga_i = (2 * b + 1) % 4
                wgx_i = (2 * b) % 6
                wga_i = (2 * b + 1) % 6
                S.dmaop("sync", lambda e, b=b, wgx_i=wgx_i: e.dma_start(
                    out=wt[wgx_i][:, :],
                    in_=wg_d.ap()[:, 2048 * b:2048 * (b + 1)]),
                    waits=[t_ag1], writes=[b_wt[wgx_i]])
                S.dmaop("sync", lambda e, b=b, wga_i=wga_i: e.dma_start(
                    out=wt[wga_i][:, :],
                    in_=wg_d.ap()[:, 2048 * (16 + b):2048 * (17 + b)]),
                    waits=[t_ag1], writes=[b_wt[wga_i]])
                for bank_i, wi in ((gx_i, wgx_i), (ga_i, wga_i)):
                    mm_group(bank_i,
                             (lambda wi: lambda k:
                              wt[wi][:, k * 128:(k + 1) * 128])(wi),
                             lambda k: ln1[:, k * T:(k + 1) * T],
                             [b_wt[wi], b_ln1], NBD)
                tg, tgB = rot[2 + b % 2], b_rot[2 + b % 2]
                tg2, tg2B = rot[9], b_rot[9]
                gs, gsB = rot[4], b_rot[4]
                af_, afB = rot[6 + b % 2], b_rot[6 + b % 2]
                la_, laB = rot[8], b_rot[8]
                a2_, a2B = rot[2 + b % 2], b_rot[2 + b % 2]
                C_, CB = rot[11 + b % 2], b_rot[11 + b % 2]
                ec_, ecB = rot[10], b_rot[10]
                S.op("scalar", lambda e, gx_i=gx_i, tg=tg: e.activation(
                    tg[:, :], psum[gx_i][:, :], AF.Tanh, scale=0.5),
                    reads=[b_ps[gx_i]], writes=[tgB])
                S.op("gpsimd", lambda e, tg=tg, gs=gs: e.tensor_scalar(
                    gs[:, :], tg[:, :], 0.5, 0.5, OP.mult, OP.add),
                    reads=[tgB], writes=[gsB])
                S.op("scalar", lambda e, ga_i=ga_i, tg2=tg2: e.activation(
                    tg2[:, :], psum[ga_i][:, :], AF.Tanh, scale=0.5),
                    reads=[b_ps[ga_i]], writes=[tg2B])
                S.op("scalar", lambda e, tg2=tg2, af_=af_: e.activation(
                    af_[:, :], tg2[:, :], AF.Exp, scale=hsneg[:, 0:1],
                    bias=hsneg[:, 0:1]),
                    waits=[t_sneg], reads=[tg2B], writes=[afB])
                S.op("vector", lambda e, tg2=tg2, la_=la_: e.tensor_scalar(
                    la_[:, :], tg2[:, :], hsneg[:, 0:1], hsneg[:, 0:1],
                    OP.mult, OP.add),
                    waits=[t_sneg], reads=[tg2B], writes=[laB])
                cin = 0.0 if j == 0 else carC_t[(j + 1) % 2][:, b:b + 1]
                crd = [] if j == 0 else [b_carC[(j + 1) % 2]]
                S.op("vector", lambda e, la_=la_, cin=cin, C_=C_:
                     e.tensor_tensor_scan(C_[:, :], la_[:, :], zeros_t[:, :],
                                          cin, OP.add, OP.add),
                     waits=[t_const], reads=[laB] + crd, writes=[CB])
                if j + 1 < NCH:
                    S.op("vector", lambda e, b=b, j=j, C_=C_: e.tensor_copy(
                        carC_t[j % 2][:, b:b + 1], C_[:, 511:512]),
                        reads=[CB], writes=[b_carC[j % 2]])
                S.op("scalar", lambda e, C_=C_, ec_=ec_: e.activation(
                    ec_[:, :], C_[:, :], AF.Exp), reads=[CB], writes=[ecB])
                S.op("scalar", lambda e, af_=af_, a2_=a2_: e.activation(
                    a2_[:, :], af_[:, :], AF.Square),
                    reads=[afB], writes=[a2B])
                S.op("vector", lambda e, b=b, gs=gs: e.tensor_mul(
                    bxs[:, b * T:(b + 1) * T], gs[:, :],
                    ln1[:, b * T:(b + 1) * T]),
                    reads=[gsB, b_ln1], writes=[b_bxs])
                S.op("gpsimd", lambda e, b=b, af_=af_: e.tensor_copy(
                    abf[:, b * T:(b + 1) * T], af_[:, :]),
                    reads=[afB], writes=[b_abf])
                S.op("vector", lambda e, b=b, a2_=a2_: e.tensor_scalar(
                    ybf[:, b * T:(b + 1) * T], a2_[:, :], -1.0, 1.0,
                    OP.mult, OP.add), reads=[a2B], writes=[b_ybf])
                estore.append(S.dmaop(
                    "gpsimd", lambda e, b=b, j=j, ec_=ec_: e.dma_start(
                        out=e_d.ap()[b * 128:(b + 1) * 128,
                                     j * T:(j + 1) * T],
                        in_=ec_[:, :]), reads=[ecB]))
                if b < 8:
                    emit_qstep(qpp0, b)
                else:
                    emit_qstep(qpp1, b - 8)
                    if b >= 14:
                        emit_qstep(qpp2, b - 14)

            if j == 2:
                t_ag2 = emit_ag2()

            if j + 1 < NCH:
                ln1_apply(j + 1)
                if j + 2 < NCH:
                    xa_load(j + 2, xs_d, [])

            for b in range(NBD):
                lnv, lnvB = rot[2 + b % 2], b_rot[2 + b % 2]
                mlt, mltB = rot[5 + 5 * (b % 2)], b_rot[5 + 5 * (b % 2)]
                bfv, bfvB = (rot[8], b_rot[8]) if b % 2 == 0 else \
                    (invr_t, b_invr)
                a32, a32B = rot[9 + (b % 2) * 4 - (b % 2) * 9 + (b % 2) * 4]\
                    if False else (rot[4 if b % 2 else 9],
                                   b_rot[4 if b % 2 else 9])
                a32, a32B = (rot[4], b_rot[4]) if b % 2 else \
                    (rot[9], b_rot[9])
                hh, hhB = rot[6 + b % 2], b_rot[6 + b % 2]
                S.op("scalar", lambda e, b=b, lnv=lnv: e.activation(
                    lnv[:, :], ybf[:, b * T:(b + 1) * T], AF.Ln),
                    reads=[b_ybf], writes=[lnvB])
                S.op("scalar", lambda e, lnv=lnv, mlt=mlt: e.activation(
                    mlt[:, :], lnv[:, :], AF.Exp, scale=0.5),
                    reads=[lnvB], writes=[mltB])
                S.op("gpsimd", lambda e, b=b, mlt=mlt, bfv=bfv:
                     e.tensor_mul(bfv[:, :], mlt[:, :],
                                  bxs[:, b * T:(b + 1) * T]),
                     reads=[mltB, b_bxs], writes=[bfvB])
                S.op("vector", lambda e, b=b, a32=a32: e.tensor_copy(
                    a32[:, :], abf[:, b * T:(b + 1) * T]),
                    reads=[b_abf], writes=[a32B])
                hin = 0.0 if j == 0 else carh_t[(j + 1) % 2][:, b:b + 1]
                hrd = [] if j == 0 else [b_carh[(j + 1) % 2]]
                S.op("vector", lambda e, a32=a32, bfv=bfv, hin=hin, hh=hh:
                     e.tensor_tensor_scan(hh[:, :], a32[:, :], bfv[:, :],
                                          hin, OP.mult, OP.add),
                     reads=[a32B, bfvB] + hrd, writes=[hhB])
                if j + 1 < NCH:
                    S.op("vector", lambda e, b=b, j=j, hh=hh: e.tensor_copy(
                        carh_t[j % 2][:, b:b + 1], hh[:, 511:512]),
                        reads=[hhB], writes=[b_carh[j % 2]])
                else:
                    S.op("vector", lambda e, b=b, hh=hh: e.tensor_copy(
                        cart_t[:, b:b + 1], hh[:, 511:512]),
                        reads=[hhB], writes=[b_cart])
                hstore.append(S.dmaop(
                    "gpsimd", lambda e, b=b, j=j, hh=hh: e.dma_start(
                        out=h_d.ap()[b * 128:(b + 1) * 128,
                                     j * T:(j + 1) * T],
                        in_=hh[:, :]), reads=[hhB]))
                if b < 6:
                    emit_qstep(qpp2, b + 2)
            if j == NCH - 1:
                t_ag3 = emit_ag3()

        t_cart = S.dmaop("gpsimd", lambda e: e.dma_start(
            out=carc_d.ap(), in_=cart_t[:, :]), reads=[b_cart])
        t_ag4 = S.op("gpsimd", lambda e: e.collective_compute(
            "AllGather", OP.bypass, replica_groups=[list(range(NCORES))],
            ins=[carc_d.ap()], outs=[carg_d.ap()]), waits=[t_cart], sem="cc")
        S.dmaop("sync", lambda e: e.dma_start(
            out=agc_t,
            in_=carg_d.ap().rearrange("(r p) b -> r (p b)", r=8)),
            waits=[t_ag4], writes=[b_xa])
        for k in range(4):
            S.op("tensor", lambda e, k=k: e.matmul(
                psum[5][0:1, :], selt[:, 0:1],
                agc_t[:, k * 512:(k + 1) * 512], start=True, stop=True),
                waits=[t_sel] + ([b_ps[5].w] if b_ps[5].w else []) +
                list(b_ps[5].r), reads=[b_xa], writes=[b_ps[5]])
            S.op("vector", lambda e, k=k: e.tensor_copy(
                crow[0:1, k * 512:(k + 1) * 512], psum[5][0:1, :]),
                reads=[b_ps[5]], writes=[b_ln1])
        t_c1 = S.dmaop("gpsimd", lambda e: e.dma_start(
            out=car1_d.ap(), in_=crow[0:1, :]), reads=[b_ln1])
        S.dmaop("sync", lambda e: e.dma_start(
            out=carry_t[:, :],
            in_=car1_d.ap().rearrange("o (ql e qh) -> o qh ql e",
                                      ql=8, e=16, qh=16),
            allow_slow_non_contiguous=True),
            waits=[t_c1], writes=[b_carry])

        gate_all = [S.tok(s) for s in ("pe", "dv", "ac", "po")]
        ffn_gate = gate_all + S.qtoks("qs") + S.qtoks("qc") + S.qtoks("qg")

        FBS, FBQ, FBM = 3, 5, 6
        xmst = {}

        def f_stats_tasks(src_fn, src_buf):
            ts = []
            for k in range(10):
                def t(k=k):
                    if k >= 2:
                        for bb in (2 * (k - 2), 2 * (k - 2) + 1):
                            stats_mm(src_fn(bb), src_buf, bb,
                                     bb == 0, bb == NBD - 1,
                                     bs=FBS, bq=FBQ)
                    if k < 8:
                        for bb in (2 * k, 2 * k + 1):
                            stats_cpsq(src_fn(bb), src_buf, bb)
                ts.append(t)
            return ts

        def f_rows_tasks():
            def t0():
                ln_rows(DM, bs=FBS, bq=FBQ)

            def t1():
                bcast(FBM, rowA)
                bcast(FBS, rowB)
            return [t0, t1]

        def s1_tasks(jc, gate, batch_e=False):
            ts = []

            def t0(jc=jc, gate=gate):
                xa_load(jc, h_d.ap(),
                        list(gate) + hstore[16 * jc:16 * (jc + 1)],
                        qlay=True)
                if batch_e:
                    S.dmaop("scalar", lambda e, jc=jc: e.dma_start(
                        out=ffn[:, :].rearrange("p (c t) -> p c t", c=NBD),
                        in_=e_d.ap().rearrange("(q c) t -> q c t", c=NBD)
                            [:, :, jc * T:(jc + 1) * T]),
                        waits=list(gate) + estore[16 * jc:16 * (jc + 1)],
                        writes=[b_ffn])
            ts.append(t0)
            for k in range(8):
                def t(k=k, jc=jc, gate=gate):
                    for bb in (2 * k, 2 * k + 1):
                        if batch_e:
                            S.op("vector", lambda e, bb=bb:
                                 e.scalar_tensor_tensor(
                                xa[:, bb * T:(bb + 1) * T],
                                ffn[:, bb * T:(bb + 1) * T],
                                carry_t[:, bb:bb + 1],
                                xa[:, bb * T:(bb + 1) * T],
                                OP.mult, OP.add),
                                reads=[b_ffn, b_carry], writes=[b_xa])
                            continue
                        eb, ebB = rot[bb % 2], b_rot[bb % 2]
                        S.dmaop("scalar", lambda e, bb=bb, eb=eb, jc=jc:
                                e.dma_start(
                            out=eb[:, :],
                            in_=e_d.ap().rearrange(
                                "(q e) t -> q e t", e=NBD)
                                [:, bb:bb + 1, jc * T:(jc + 1) * T]),
                            waits=(list(gate) if bb == 0 else []) +
                            estore[16 * jc:16 * (jc + 1)],
                            writes=[ebB])
                        S.op("vector", lambda e, bb=bb, eb=eb:
                             e.scalar_tensor_tensor(
                            xa[:, bb * T:(bb + 1) * T], eb[:, :],
                            carry_t[:, bb:bb + 1],
                            xa[:, bb * T:(bb + 1) * T], OP.mult, OP.add),
                            reads=[ebB, b_carry], writes=[b_xa])
                ts.append(t)
            return ts

        def s2_tasks(jc, gate, batch_x=False):
            xmst[jc] = []
            ts = f_stats_tasks(lambda bb: xa[:, bb * T:(bb + 1) * T], b_xa)
            if batch_x:
                def tx(jc=jc):
                    S.dmaop("scalar", lambda e, jc=jc: e.dma_start(
                        out=ffn[:, :].rearrange("p (c t) -> p c t", c=NBD),
                        in_=xs_d.rearrange("(q c) t -> q c t", c=NBD)
                            [:, :, jc * T:(jc + 1) * T]),
                        writes=[b_ffn])
                ts.append(tx)
            ts += f_rows_tasks()
            for k in range(8):
                def t(k=k, jc=jc, gate=gate):
                    for bb in (2 * k, 2 * k + 1):
                        rr, rb = rot[8 + bb % 2], b_rot[8 + bb % 2]
                        if batch_x:
                            xb, xbB = ffn[:, bb * T:(bb + 1) * T], b_ffn
                        else:
                            xb, xbB = rot[bb % 2], b_rot[bb % 2]
                            S.dmaop("scalar", lambda e, bb=bb, xb=xb, jc=jc:
                                    e.dma_start(
                                out=xb[:, :],
                                in_=xs_d.rearrange(
                                    "(q e) t -> q e t", e=NBD)
                                    [:, bb:bb + 1, jc * T:(jc + 1) * T]),
                                waits=(list(gate) if bb == 0 else []),
                                writes=[xbB])
                        S.op("vector", lambda e, bb=bb, rr=rr: e.tensor_sub(
                            rr[:, :], xa[:, bb * T:(bb + 1) * T],
                            psum[FBM][:, :]),
                            reads=[b_xa, b_ps[FBM]], writes=[rb])
                        S.op("vector", lambda e, rr=rr: e.tensor_mul(
                            rr[:, :], rr[:, :], psum[FBS][:, :]),
                            reads=[b_ps[FBS]], writes=[rb])
                        S.op("vector", lambda e, bb=bb, rr=rr, xb=xb,
                             xbB=xbB: e.tensor_add(
                            xa[:, bb * T:(bb + 1) * T], rr[:, :], xb[:, :]),
                            reads=[rb, xbB], writes=[b_xa])
                        xmst[jc].append(S.dmaop(
                            "gpsimd", lambda e, bb=bb, jc=jc: e.dma_start(
                                out=xm_d.ap().rearrange(
                                    "(q e) t -> q e t", e=NBD)
                                    [:, bb:bb + 1, jc * T:(jc + 1) * T],
                                in_=xa[:, bb * T:(bb + 1) * T]),
                            reads=[b_xa]))
                ts.append(t)
            return ts

        def s3_tasks(jc, gate):
            par = jc % 2
            ts = f_stats_tasks(lambda bb: xa[:, bb * T:(bb + 1) * T], b_xa)
            ts += f_rows_tasks()
            for k in range(8):
                def t(k=k, gate=gate, par=par):
                    for bb in (2 * k, 2 * k + 1):
                        rr, rb = rot[8 + bb % 2], b_rot[8 + bb % 2]
                        S.op("vector", lambda e, bb=bb, rr=rr: e.tensor_sub(
                            rr[:, :], xa[:, bb * T:(bb + 1) * T],
                            psum[FBM][:, :]),
                            reads=[b_xa, b_ps[FBM]], writes=[rb])
                        S.op("vector", lambda e, bb=bb, rr=rr, par=par:
                             e.tensor_mul(
                            ln2_p[par][:, bb * T:(bb + 1) * T], rr[:, :],
                            psum[FBS][:, :]),
                            waits=(list(gate) if bb == 0 and k == 0
                                   else []),
                            reads=[rb, b_ps[FBS]], writes=[b_ln2[par]])
                ts.append(t)
            return ts

        def ep_tasks(jc, batch_m=False):
            ts = f_stats_tasks(lambda bb: xa[:, bb * T:(bb + 1) * T], b_xa)
            if batch_m:
                pe_tok = S.tok("pe")

                def tm(jc=jc, pe_tok=pe_tok):
                    S.dmaop("scalar", lambda e, jc=jc: e.dma_start(
                        out=ffn[:, :].rearrange("p (c t) -> p c t", c=NBD),
                        in_=xm_d.ap().rearrange("(q c) t -> q c t", c=NBD)
                            [:, :, jc * T:(jc + 1) * T]),
                        waits=[pe_tok] + xmst[jc], writes=[b_ffn])
                ts.insert(0, tm)
            ts += f_rows_tasks()
            for k in range(8):
                def t(k=k, jc=jc):
                    for bb in (2 * k, 2 * k + 1):
                        rr, rb = rot[8 + bb % 2], b_rot[8 + bb % 2]
                        oo, ooB = rot[10 + bb % 2], b_rot[10 + bb % 2]
                        if batch_m:
                            xm, xmB = ffn[:, bb * T:(bb + 1) * T], b_ffn
                        else:
                            xm, xmB = rot[bb % 2], b_rot[bb % 2]
                            S.dmaop("scalar", lambda e, bb=bb, xm=xm, jc=jc:
                                    e.dma_start(
                                out=xm[:, :],
                                in_=xm_d.ap().rearrange(
                                    "(q e) t -> q e t", e=NBD)
                                    [:, bb:bb + 1, jc * T:(jc + 1) * T]),
                                waits=xmst[jc], writes=[xmB])
                        S.op("vector", lambda e, bb=bb, rr=rr: e.tensor_sub(
                            rr[:, :], xa[:, bb * T:(bb + 1) * T],
                            psum[FBM][:, :]),
                            reads=[b_xa, b_ps[FBM]], writes=[rb])
                        S.op("vector", lambda e, rr=rr: e.tensor_mul(
                            rr[:, :], rr[:, :], psum[FBS][:, :]),
                            reads=[b_ps[FBS]], writes=[rb])
                        S.op("vector", lambda e, rr=rr, xm=xm, oo=oo,
                             xmB=xmB:
                             e.tensor_add(oo[:, :], rr[:, :], xm[:, :]),
                            reads=[rb, xmB], writes=[ooB])
                        S.dmaop("gpsimd", lambda e, bb=bb, jc=jc, oo=oo:
                                e.dma_start(
                            out=out_d.rearrange("(q e) t -> q e t", e=NBD)
                                [:, bb:bb + 1, jc * T:(jc + 1) * T],
                            in_=oo[:, :]), reads=[ooB])
                ts.append(t)
            return ts

        for t in (s1_tasks(0, ffn_gate, batch_e=True) +
                  s2_tasks(0, ffn_gate, batch_x=True) +
                  s3_tasks(0, ffn_gate)):
            t()

        for j in range(NCH):
            tasks = []
            if j > 0:
                tasks += ep_tasks(j - 1)
            if j + 1 < NCH:
                tasks += s1_tasks(j + 1, [])
                tasks += s2_tasks(j + 1, [])
                tasks += s3_tasks(j + 1, ffn_gate if j == 0 else [])
            ti = 0
            ln2c, ln2cB = ln2_p[j % 2], b_ln2[j % 2]
            def sumsq_mm(bb):
                sq, sqB = sqg[bb % 2], b_sqg[bb % 2]
                first, last = (bb == 0), (bb == NBH - 1)
                bw = ([b_ps[4].w] if first and b_ps[4].w else []) + \
                    (list(b_ps[4].r) if first else [])
                S.op("tensor", lambda e, sq=sq, first=first, last=last:
                     e.matmul(psum[4][0:1, :], onesb[:, 0:1], sq[:, :],
                              start=first, stop=last),
                     waits=bw, reads=[sqB],
                     writes=[b_ps[4]] if last else [])
                if not last:
                    b_ps[4].w = S.tok("pe")

            for beta in range(NBH):
                if beta >= 2:
                    sumsq_mm(beta - 2)
                if ti < len(tasks):
                    tasks[ti]()
                    ti += 1
                wg_i = 2 * (beta % 3)
                wu_i = wg_i + 1
                gb_i = beta % 2
                ub_i = 2 if beta % 2 == 0 else 7
                S.dmaop("sync", lambda e, beta=beta, wg_i=wg_i:
                        e.dma_start(
                    out=wt[wg_i][:, :],
                    in_=gq_d.ap()[:, 2048 * beta:2048 * (beta + 1)]),
                    waits=[t_ag2], writes=[b_wt[wg_i]])
                S.dmaop("sync", lambda e, beta=beta, wu_i=wu_i:
                        e.dma_start(
                    out=wt[wu_i][:, :],
                    in_=gq_d.ap()[:, 2048 * (64 + beta):2048 * (65 + beta)]),
                    waits=[t_ag2], writes=[b_wt[wu_i]])
                for bank_i, wi in ((gb_i, wg_i), (ub_i, wu_i)):
                    mm_group(bank_i,
                             (lambda wi: lambda k:
                              wt[wi][:, k * 128:(k + 1) * 128])(wi),
                             (lambda l2: lambda k:
                              l2[:, k * T:(k + 1) * T])(ln2c),
                             [b_wt[wi], ln2cB], NBD)
                tg, tgB = rot[6 + beta % 2], b_rot[6 + beta % 2]
                sg, sgB = rot[4 + beta % 2], b_rot[4 + beta % 2]
                tt, ttB = rot[2 + beta % 2], b_rot[2 + beta % 2]
                S.op("scalar", lambda e, gb_i=gb_i, tg=tg: e.activation(
                    tg[:, :], psum[gb_i][:, :], AF.Tanh, scale=0.5),
                    reads=[b_ps[gb_i]], writes=[tgB])
                S.op("vector", lambda e, tg=tg, sg=sg: e.tensor_scalar(
                    sg[:, :], tg[:, :], 0.5, 0.5, OP.mult, OP.add),
                    reads=[tgB], writes=[sgB])
                S.op("vector", lambda e, gb_i=gb_i, sg=sg, tt=tt:
                     e.tensor_mul(tt[:, :], psum[gb_i][:, :], sg[:, :]),
                     reads=[sgB, b_ps[gb_i]], writes=[ttB])
                hb_t = hb[beta // 16]
                hb_sl = slice((beta % 16) * T, (beta % 16 + 1) * T)
                hbB = b_hb[beta // 16]
                S.op("vector", lambda e, ub_i=ub_i, tt=tt, hb_t=hb_t,
                     hb_sl=hb_sl: e.tensor_mul(
                    hb_t[:, hb_sl], tt[:, :], psum[ub_i][:, :]),
                    reads=[ttB, b_ps[ub_i]], writes=[hbB])
                sq, sqB = sqg[beta % 2], b_sqg[beta % 2]
                S.op("gpsimd", lambda e, sq=sq, hb_t=hb_t, hb_sl=hb_sl:
                     e.tensor_mul(sq[:, :], hb_t[:, hb_sl], hb_t[:, hb_sl]),
                     reads=[hbB], writes=[sqB])
            sumsq_mm(NBH - 2)
            sumsq_mm(NBH - 1)
            while ti < len(tasks):
                tasks[ti]()
                ti += 1
            S.op("scalar", lambda e: e.activation(
                rowC[0:1, :], psum[4][0:1, :], AF.Ln,
                bias=cst[0:1, 1:2], scale=1.0 / HF),
                waits=[t_const], reads=[b_ps[4]], writes=[b_rows])
            S.op("scalar", lambda e: e.activation(
                rowB[0:1, :], rowC[0:1, :], AF.Exp, scale=-0.5),
                reads=[b_rows], writes=[b_rows])
            bcast(FBS, rowB)
            S.op("vector", lambda e: e.tensor_copy(
                invr_t[:, :], psum[FBS][:, :]), reads=[b_ps[FBS]],
                writes=[b_invr])
            for hp in range(2):
                for beta in range(NBH):
                    wq_i = beta % 3
                    S.dmaop("sync", lambda e, beta=beta, hp=hp, wq_i=wq_i:
                            e.dma_start(
                        out=wqp[wq_i][:, :],
                        in_=pq_d.ap()[beta * 128:(beta + 1) * 128,
                                      hp * 1024:(hp + 1) * 1024]),
                        waits=[t_ag3], writes=[b_wqp[wq_i]])
                    hb_t = hb[beta // 16]
                    hb_sl = slice((beta % 16) * T, (beta % 16 + 1) * T)
                    hbB = b_hb[beta // 16]
                    for dk in range(8):
                        first = (beta == 0)
                        lastg = (beta == NBH - 1)
                        tr = (dk == 7) or lastg
                        waits = []
                        if first:
                            if b_ps[dk].w:
                                waits.append(b_ps[dk].w)
                            waits.extend(b_ps[dk].r)
                        if dk == 0:
                            if b_wqp[wq_i].w:
                                waits.append(b_wqp[wq_i].w)
                            if hbB.w:
                                waits.append(hbB.w)
                        S.op("tensor", lambda e, dk=dk, wq_i=wq_i,
                             hb_t=hb_t, hb_sl=hb_sl, first=first,
                             lastg=lastg: e.matmul(
                            psum[dk][:, :],
                            wqp[wq_i][:, dk * 128:(dk + 1) * 128],
                            hb_t[:, hb_sl], start=first, stop=lastg),
                            waits=waits,
                            reads=[b_wqp[wq_i], hbB] if dk == 7 else [],
                            writes=[b_ps[dk]] if lastg else [],
                            track=tr)
                        if not lastg:
                            b_ps[dk].w = S.tok("pe")
                for dk in range(8):
                    db = hp * 8 + dk
                    S.op("vector", lambda e, dk=dk, db=db: e.tensor_mul(
                        xa[:, db * T:(db + 1) * T], psum[dk][:, :],
                        invr_t[:, :]),
                        reads=[b_ps[dk], b_invr], writes=[b_xa])

        for t in ep_tasks(NCH - 1, batch_m=True):
            t()

        S.op("sync", lambda e: None,
             waits=S.qtoks("qs") + S.qtoks("qc") + S.qtoks("qg"),
             track=False)

        with nc.Block() as block:
            @block.sync
            def _(eng):
                S.replay("sync", eng, sems)

            @block.scalar
            def _(eng):
                S.replay("scalar", eng, sems)

            @block.vector
            def _(eng):
                S.replay("vector", eng, sems)

            @block.gpsimd
            def _(eng):
                S.replay("gpsimd", eng, sems)

            @block.tensor
            def _(eng):
                S.replay("tensor", eng, sems)

    return nc


_NC_CACHE = {}


def kernel(L0_attn_x, L0_attn_W_in, L0_attn_a_param, L0_ffn_w_gate_up,
           L0_ffn_w_proj, debug=False):
    xT = np.ascontiguousarray(np.transpose(np.asarray(L0_attn_x), (0, 2, 1)))
    W_inT = np.asarray(L0_attn_W_in).T
    W2 = np.ascontiguousarray(
        W_inT.reshape(16, 128, 32, 128).transpose(1, 2, 0, 3))
    wguT = np.ascontiguousarray(np.asarray(L0_ffn_w_gate_up).T)
    wpT = np.asarray(L0_ffn_w_proj).T
    wpT2 = np.ascontiguousarray(
        wpT.reshape(HF, 128, 16).transpose(0, 2, 1).reshape(HF, DM))
    apar_h = np.ascontiguousarray(
        np.asarray(L0_attn_a_param, np.float32).reshape(128, 1))

    expq_h = np.zeros((2, 128), ml_dtypes.bfloat16)
    expq_h[0, 0:64] = 1
    expq_h[1, 64:128] = 1
    in_maps = []
    for c in range(NCORES):
        b, half = c // 2, c % 2
        sel = np.zeros((8, 1), np.float32)
        if half == 1:
            sel[c - 1, 0] = 1.0
        in_maps.append({
            "xs": np.ascontiguousarray(xT[b, :, half * TOK:(half + 1) * TOK]),
            "wslab": np.ascontiguousarray(
                W2[16 * c:16 * (c + 1)].reshape(16, 16 * FD)),
            "gslab": np.ascontiguousarray(wguT[256 * c:256 * (c + 1)]),
            "pslab": np.ascontiguousarray(wpT2[1024 * c:1024 * (c + 1)]),
            "apar": apar_h,
            "sel": sel,
            "expq_in": expq_h,
        })
    key = ("main", debug)
    if key not in _NC_CACHE:
        _NC_CACHE[key] = build(debug=debug)
    nc = _NC_CACHE[key]
    res = run_bass_kernel_spmd(nc, in_maps, core_ids=list(range(NCORES)))
    out = np.empty((BSZ, SEQ, DM), np.float32)
    for c in range(NCORES):
        b, half = c // 2, c % 2
        out[b, half * TOK:(half + 1) * TOK, :] = res.results[c]["outT"].T
    if debug:
        return out, res
    return out


